# version 09
# baseline (speedup 1.0000x reference)
import numpy as np
import ml_dtypes
import concourse.bass as bass
import concourse.mybir as mybir
from concourse.bass_utils import run_bass_kernel_spmd
from contextlib import ExitStack

F32 = mybir.dt.float32
BF16 = mybir.dt.bfloat16
Alu = mybir.AluOpType
AFT = mybir.ActivationFunctionType
AX = mybir.AxisListType

S, D, H = 2048, 1024, 16
NT = S // 128
KC = D // 128
NP = H // 2
DFF = 2816
NJ = DFF // 128

_MIDS = [0.25, 0.75, 1.25, 1.75, 2.5, 3.5, 5.0]
TH = [float(np.float32((m / 6.0) ** 2)) for m in _MIDS]


class Sem:
    def __init__(self, handle):
        self.h = handle
        self.n = 0


class Sched:
    ENGS = ("sync", "tensor", "scalar", "vector", "gpsimd")

    def __init__(self):
        self.steps = {e: [] for e in self.ENGS}

    def add(self, eng, emit, waits=(), inc=None, inc_amt=1):
        self.steps[eng].append((list(waits), emit, inc, inc_amt))
        if inc is not None:
            inc.n += inc_amt
            return inc.n
        return None

    def run(self, block):
        for eng in self.ENGS:
            steps = self.steps[eng]
            if not steps:
                continue

            def mk(steps):
                def f(ctx):
                    for waits, emit, inc, amt in steps:
                        for sem, val in waits:
                            ctx.wait_ge(sem.h, val)
                        ins = emit(ctx)
                        if inc is not None:
                            ins.then_inc(inc.h, amt)
                return f
            getattr(block, eng)(mk(steps))
        self.steps = {e: [] for e in self.ENGS}


def build_nc(debug=False):
    nc = bass.Bass("TRN2", target_bir_lowering=False)

    x_h = nc.dram_tensor("x", [S, D], F32, kind="ExternalInput").ap()
    wqkv_h = nc.dram_tensor("wqkv", [D, NP * 384], F32,
                            kind="ExternalInput").ap()
    wo_h = nc.dram_tensor("wo", [D, D], F32, kind="ExternalInput").ap()
    bqkv_h = nc.dram_tensor("bqkv", [1, NP * 384], F32,
                            kind="ExternalInput").ap()
    bo_h = nc.dram_tensor("bo", [1, D], F32, kind="ExternalInput").ap()
    wg_h = nc.dram_tensor("wg", [DFF, D], F32, kind="ExternalInput").ap()
    wu_h = nc.dram_tensor("wu", [DFF, D], F32, kind="ExternalInput").ap()
    wd_h = nc.dram_tensor("wd", [D, DFF], F32, kind="ExternalInput").ap()
    cos_h = nc.dram_tensor("cosv", [S, 32], F32, kind="ExternalInput").ap()
    sin_h = nc.dram_tensor("sinv", [S, 32], F32, kind="ExternalInput").ap()
    idn_h = nc.dram_tensor("idn", [128, 128], F32, kind="ExternalInput").ap()
    msk_h = nc.dram_tensor("msk", [128, 4 * 512], F32, kind="ExternalInput").ap()
    out_h = nc.dram_tensor("out", [S, D], F32, kind="ExternalOutput").ap()
    xscr = nc.dram_tensor("xscr", [S, D], F32, kind="Internal").ap()
    wgq_raw = nc.dram_tensor("wgq_raw", [DFF, D], BF16, kind="Internal").ap()
    wuq_raw = nc.dram_tensor("wuq_raw", [DFF, D], BF16, kind="Internal").ap()
    wdq_raw = nc.dram_tensor("wdq_raw", [D, DFF], BF16, kind="Internal").ap()
    x_v = x_h.rearrange("(n p) d -> p n d", p=128)
    xscr_v = xscr.rearrange("(n p) d -> p n d", p=128)
    out_v = out_h.rearrange("(n p) d -> p n d", p=128)

    root = ExitStack()

    def mksb(st, name, shape, dtype=F32, side=None):
        kw = {"side": side} if side else {}
        return st.enter_context(nc.sbuf_tensor(name, shape, dtype, **kw))

    R = "right"

    sDMA = Sem(root.enter_context(nc.semaphore("sDMA")))
    sSCR = Sem(root.enter_context(nc.semaphore("sSCR")))
    sPE = Sem(root.enter_context(nc.semaphore("sPE")))
    sACT = Sem(root.enter_context(nc.semaphore("sACT")))
    sDVE = Sem(root.enter_context(nc.semaphore("sDVE")))
    sPOOL = Sem(root.enter_context(nc.semaphore("sPOOL")))

    psA = root.enter_context(nc.psum_tensor("psA", [128, 512], F32))
    psB = root.enter_context(nc.psum_tensor("psB", [128, 512], F32))
    psC = root.enter_context(nc.psum_tensor("psC", [128, 512], F32))
    psD = root.enter_context(nc.psum_tensor("psD", [128, 512], F32))
    psO = root.enter_context(nc.psum_tensor("psO", [128, 1024], F32))
    psT1 = root.enter_context(nc.psum_tensor("psT1", [128, 1024], BF16))
    psT2 = root.enter_context(nc.psum_tensor("psT2", [128, 1024], BF16))
    psTs = [psT1, psT2]

    idn = mksb(root, "idn_b", [128, 128], BF16)
    ones1 = mksb(root, "ones1", [1, 128], BF16)
    epsc = mksb(root, "epsc", [128, 1], F32)
    nb12 = mksb(root, "nb12", [128, 1], F32)
    rstd_all = mksb(root, "rstd_all", [128, NT], F32)
    negm_all = mksb(root, "negm_all", [128, NT], F32)

    sch = Sched()

    def dma(eng, sem, out, in_, waits=()):
        def emit(ctx, out=out, in_=in_):
            return ctx.dma_start(out=out, in_=in_)
        return sch.add(eng, emit, waits=waits, inc=sem, inc_amt=16)

    dbg_outs = []

    def dbg(name, shape, dtype, src_ap, waits):
        if not debug:
            return
        hap = nc.dram_tensor(name, shape, dtype, kind="ExternalOutput").ap()
        dbg_outs.append(name)
        dma("gpsimd", sSCR, hap, src_ap, waits=list(waits))

    def quant_s1(bufs, F, src, soff, waits1):
        (qu, qt_, qa, qb_, qc_, sg, s2, bm, sh, rb, bmc, rb2, xsq) = bufs
        nb = F // 16

        def c1r(v):
            sv = src.ap()[:, soff:soff + F].rearrange(
                "p (b k) -> p b k", k=16)
            return v.tensor_reduce(bm[:, 0:nb], sv, AX.X, Alu.max,
                                   apply_absolute_value=True)
        v1r = sch.add("vector", c1r, waits=list(waits1), inc=sDVE)
        aq = sch.add("scalar", (lambda c: c.activation(
            xsq[:, 0:F], src.ap()[:, soff:soff + F], AFT.Square)),
            waits=list(waits1), inc=sACT)

        def c1(v):
            v.tensor_scalar(bmc[:, 0:nb], bm[:, 0:nb], 6e-12, None, Alu.max)
            v.tensor_scalar(sh[:, 0:nb], bm[:, 0:nb], 6e-12, 1.0 / 12.0,
                            Alu.max, Alu.mult)
            return v.tensor_scalar(sg[:, 0:F], src.ap()[:, soff:soff + F],
                                   0.0, None, Alu.is_ge)
        v1a = sch.add("vector", c1, waits=[(sDVE, v1r)], inc=sDVE)
        v1 = sch.add("vector", (lambda v: v.reciprocal(
            rb[:, 0:nb], bmc[:, 0:nb])), waits=[(sDVE, v1a)], inc=sDVE)
        v1b = sch.add("vector", (lambda v: v.tensor_tensor(
            rb2[:, 0:nb], rb[:, 0:nb], rb[:, 0:nb], Alu.mult)),
            waits=[(sDVE, v1)], inc=sDVE)
        return (bufs, F, v1b, aq)

    def quant_s2(tok, dst, doff, waits3):
        (bufs, F, v1b, aq) = tok
        (qu, qt_, qa, qb_, qc_, sg, s2, bm, sh, rb, bmc, rb2, xsq) = bufs
        nb = F // 16

        def c2(v):
            rbb = rb2.ap()[:, 0:nb].unsqueeze(2).broadcast_to([128, nb, 16])
            return v.tensor_tensor(
                qu.ap()[:, 0:F].rearrange("p (b k) -> p b k", k=16),
                xsq.ap()[:, 0:F].rearrange("p (b k) -> p b k", k=16),
                rbb, Alu.mult)
        v2 = sch.add("vector", c2, waits=[(sDVE, v1b), (sACT, aq)], inc=sDVE)

        def c3(v):
            u = qu.ap()[:, 0:F]
            v.tensor_scalar(qa[:, 0:F], u, TH[0], None, Alu.is_gt)
            v.scalar_tensor_tensor(qb_[:, 0:F], u, TH[1], qa.ap()[:, 0:F],
                                   Alu.is_gt, Alu.add)
            v.scalar_tensor_tensor(qa[:, 0:F], u, TH[2], qb_.ap()[:, 0:F],
                                   Alu.is_gt, Alu.add)
            v.scalar_tensor_tensor(qb_[:, 0:F], u, TH[3], qa.ap()[:, 0:F],
                                   Alu.is_gt, Alu.add)
            v.tensor_scalar(qc_[:, 0:F], u, TH[4], None, Alu.is_gt)
            v.scalar_tensor_tensor(qc_[:, 0:F], u, TH[5], qc_.ap()[:, 0:F],
                                   Alu.is_gt, Alu.add)
            v.scalar_tensor_tensor(qc_[:, 0:F], u, TH[6], qc_.ap()[:, 0:F],
                                   Alu.is_gt, Alu.add)
            v.scalar_tensor_tensor(qc_[:, 0:F], u, TH[6], qc_.ap()[:, 0:F],
                                   Alu.is_gt, Alu.add)
            v.scalar_tensor_tensor(qu[:, 0:F], qc_.ap()[:, 0:F], 2.0,
                                   qb_.ap()[:, 0:F], Alu.mult, Alu.add)
            shb = sh.ap()[:, 0:nb].unsqueeze(2).broadcast_to([128, nb, 16])
            v.tensor_tensor(
                qt_.ap()[:, 0:F].rearrange("p (b k) -> p b k", k=16),
                qu.ap()[:, 0:F].rearrange("p (b k) -> p b k", k=16),
                shb, Alu.mult)
            v.tensor_scalar(s2[:, 0:F], sg[:, 0:F], 2.0, -1.0,
                            Alu.mult, Alu.add)
            return v.tensor_tensor(dst.ap()[:, doff:doff + F],
                                   qt_.ap()[:, 0:F],
                                   s2.ap()[:, 0:F], Alu.mult)
        return sch.add("vector", c3, waits=[(sDVE, v2)] + list(waits3),
                       inc=sDVE)

    def quant_chain(bufs, F, src, soff, dst, doff, waits1, waits3):
        tok = quant_s1(bufs, F, src, soff, waits1)
        return quant_s2(tok, dst, doff, waits3)

    with root, nc.Block() as block:
        stTmp = ExitStack()
        idn_f = mksb(stTmp, "idn_f", [128, 128], F32, side=R)
        cos_f = mksb(stTmp, "cos_f", [128, NT, 32], F32, side=R)
        sin_f = mksb(stTmp, "sin_f", [128, NT, 32], F32, side=R)
        msk_f = mksb(stTmp, "msk_f", [128, 4, 512], F32, side=R)
        bst = mksb(stTmp, "bst", [1, NP * 384], F32, side=R)

        stCst = ExitStack()
        cosb = mksb(stCst, "cosb", [128, NT, 32], BF16)
        sinb = mksb(stCst, "sinb", [128, NT, 32], BF16)
        mskb = mksb(stCst, "mskb", [128, 4, 512], BF16)
        bqkvs = mksb(stCst, "bqkvs", [1, NP * 384], BF16)
        bos = mksb(stCst, "bos", [1, D], BF16)

        dma("sync", sDMA, cos_f[:], cos_h.rearrange("(n p) j -> p n j", p=128))
        dma("sync", sDMA, sin_f[:], sin_h.rearrange("(n p) j -> p n j", p=128))
        dma("sync", sDMA, idn_f[:], idn_h)
        w_pre = dma("sync", sDMA, msk_f[:],
                    msk_h.rearrange("p (m f) -> p m f", f=512))

        def cast_consts(v):
            v.memset(epsc[:], 1e-5)
            v.memset(nb12[:], -12.0)
            v.memset(ones1[:], 1.0)
            v.tensor_copy(idn[:], idn_f[:])
            v.tensor_copy(cosb[:], cos_f[:])
            v.tensor_copy(sinb[:], sin_f[:])
            return v.tensor_copy(mskb[:], msk_f[:])
        v_consts = sch.add("vector", cast_consts, waits=[(sDMA, w_pre)],
                           inc=sDVE)

        wb1 = dma("sync", sDMA, bst[:], bqkv_h)
        ab1 = sch.add("scalar", (lambda c: c.copy(bqkvs[:], bst[:])),
                      waits=[(sDMA, wb1)], inc=sACT)
        wb2 = dma("sync", sDMA, bst.ap()[:, 0:D], bo_h, waits=[(sACT, ab1)])
        sch.add("scalar", (lambda c: c.copy(bos[:], bst.ap()[:, 0:D])),
                waits=[(sDMA, wb2)], inc=sACT)
        bias_cast_done = sACT.n
        sch.run(block)
        stTmp.close()

        stXlT = ExitStack()
        xlT = mksb(stXlT, "xlT", [128, KC, S], BF16, side=R)
        stLN = ExitStack()
        x_a = [mksb(stLN, f"x_a{i}", [128, D], F32, side=R) for i in range(2)]
        sqt = mksb(stLN, "sqt", [128, D], F32, side=R)
        mv = [mksb(stLN, f"mv{i}", [128, 6], F32, side=R) for i in range(2)]
        rinv1 = [mksb(stLN, f"rinv1_{i}", [128, 1], F32, side=R)
                 for i in range(2)]
        rstd1 = [mksb(stLN, f"rstd1_{i}", [128, 1], F32, side=R)
                 for i in range(2)]
        negm1 = [mksb(stLN, f"negm1_{i}", [128, 1], F32, side=R)
                 for i in range(2)]
        xl1 = [mksb(stLN, f"xl1_{i}", [128, D], BF16, side=R) for i in range(2)]

        evacA = {}
        aapA = {}
        vsumA = {}
        for t in range(NT):
            r = t % 2
            wx = []
            if t >= 2:
                wx += [(sACT, aapA[t - 2]), (sDVE, vsumA[t - 2])]
            wxd = dma("sync", sDMA, x_a[r][:], x_v[:, t, :], waits=wx)
            vsumA[t] = sch.add("vector", (lambda v, r=r: v.reduce_sum(
                mv[r][:, 0:1], x_a[r][:], AX.X)),
                waits=[(sDMA, wxd)], inc=sDVE)
            asq = sch.add("scalar", (lambda c, r=r: c.activation(
                sqt[:], x_a[r][:], AFT.Square, accum_out=mv[r][:, 1:2])),
                waits=[(sDMA, wxd)], inc=sACT)

            def stat2(v, r=r):
                v.tensor_scalar(mv[r][:, 2:3], mv[r][:, 0:1], 1.0 / D, None,
                                Alu.mult)
                return v.tensor_scalar(mv[r][:, 3:4], mv[r][:, 1:2], 1.0 / D,
                                       1e-5, Alu.mult, Alu.add)
            v2a = sch.add("vector", stat2,
                          waits=[(sACT, asq), (sDVE, vsumA[t])], inc=sDVE)
            v2b = sch.add("vector", (lambda v, r=r: v.tensor_tensor(
                mv[r][:, 4:5], mv[r][:, 2:3], mv[r][:, 2:3], Alu.mult)),
                waits=[(sDVE, v2a)], inc=sDVE)
            vst = sch.add("vector", (lambda v, r=r: v.tensor_tensor(
                mv[r][:, 5:6], mv[r][:, 3:4], mv[r][:, 4:5], Alu.subtract)),
                waits=[(sDVE, v2b)], inc=sDVE)
            vrc = sch.add("vector", (lambda v, r=r: v.reciprocal(
                rinv1[r][:], mv[r][:, 5:6])), waits=[(sDVE, vst)], inc=sDVE)
            ast = sch.add("scalar", (lambda c, r=r: c.activation(
                rstd1[r][:], rinv1[r][:], AFT.Sqrt)),
                waits=[(sDVE, vrc)], inc=sACT)
            vng = sch.add("vector", (lambda v, r=r: v.tensor_scalar(
                negm1[r][:], mv[r][:, 2:3], rstd1[r].ap(), -1.0,
                Alu.mult, Alu.mult)),
                waits=[(sACT, ast)], inc=sDVE)
            aapA[t] = sch.add("scalar", (lambda c, r=r: c.activation(
                xl1[r][:], x_a[r][:], AFT.Identity,
                bias=negm1[r].ap(), scale=rstd1[r].ap())),
                waits=[(sDVE, vng)], inc=sACT)

            def trsA(te, r=r):
                ins = None
                for j in range(KC):
                    ins = te.transpose(psTs[r][:, j * 128:(j + 1) * 128],
                                       xl1[r][:, j * 128:(j + 1) * 128],
                                       idn[:])
                return ins
            wtr = [(sACT, aapA[t]), (sDVE, v_consts)]
            if t >= 2:
                wtr.append((sACT, evacA[t - 2]))
            ptr = sch.add("tensor", trsA, waits=wtr, inc=sPE)
            evacA[t] = sch.add("scalar", (lambda c, t=t, r=r: c.copy(
                xlT.ap()[:, :, t * 128:(t + 1) * 128],
                psTs[r].ap().rearrange("p (j f) -> p j f", f=128))),
                waits=[(sPE, ptr)], inc=sACT)
        phaseA_act = sACT.n
        sch.run(block)
        stLN.close()

        stEq = ExitStack()
        FQ = 1024
        wfE = [mksb(stEq, f"wfE{i}", [128, FQ], F32, side=R) for i in range(2)]
        quE = mksb(stEq, "quE", [128, FQ], F32, side=R)
        qtE = mksb(stEq, "qtE", [128, FQ], F32, side=R)
        qcA = mksb(stEq, "qcA", [128, FQ], F32, side=R)
        qcB = mksb(stEq, "qcB", [128, FQ], F32, side=R)
        qcC = mksb(stEq, "qcC", [128, FQ], F32, side=R)
        qsg = mksb(stEq, "qsg", [128, FQ], BF16, side=R)
        qs2 = mksb(stEq, "qs2", [128, FQ], BF16, side=R)
        qqE = [mksb(stEq, f"qqE{i}", [128, FQ], BF16, side=R)
               for i in range(2)]
        qbm = mksb(stEq, "qbm", [128, FQ // 16], F32, side=R)
        qsh = mksb(stEq, "qsh", [128, FQ // 16], F32, side=R)
        qrb = mksb(stEq, "qrb", [128, FQ // 16], F32, side=R)
        qbc = mksb(stEq, "qbc", [128, FQ // 16], F32, side=R)
        qr2 = mksb(stEq, "qr2", [128, FQ // 16], F32, side=R)
        qxs = mksb(stEq, "qxs", [128, FQ], F32, side=R)
        bufsE = (quE, qtE, qcA, qcB, qcC, qsg, qs2, qbm, qsh, qrb, qbc,
                 qr2, qxs)

        wjobs = []
        for rt in range(NJ):
            wjobs.append((wg_h, wgq_raw, rt, 0, 1024))
        for rt in range(NJ):
            wjobs.append((wu_h, wuq_raw, rt, 0, 1024))
        for rt in range(KC):
            for cc, F in ((0, 1024), (1024, 1024), (2048, 768)):
                wjobs.append((wd_h, wdq_raw, rt, cc, F))

        e1 = {"ji": 0, "stage2": None, "prev_chain": 0, "out_dma": {},
              "aq_prev": {}}
        a_done_dve = sDVE.n

        def e1_stage1():
            ji = e1["ji"]
            if ji >= len(wjobs):
                return
            wh, raw, rt, cc, F = wjobs[ji]
            r = ji % 2
            ws = ([(sDVE, e1["prev_chain"])] if ji >= 1
                  else [(sACT, phaseA_act), (sDVE, a_done_dve)])
            wdma = dma("sync", sDMA, wfE[r][:, 0:F],
                       wh[rt * 128:(rt + 1) * 128, cc:cc + F], waits=ws)
            tok = quant_s1(bufsE, F, wfE[r], 0, [(sDMA, wdma)])
            e1["stage2"] = (ji, raw, rt, cc, F, tok)
            e1["ji"] = ji + 1

        def e1_do_stage2():
            if e1["stage2"] is None:
                return
            ji, raw, rt, cc, F, tok = e1["stage2"]
            e1["stage2"] = None
            qr = ji % 2
            w3 = []
            if ji >= 2 and (ji - 2) in e1["out_dma"]:
                w3.append((sSCR, e1["out_dma"][ji - 2]))
            qv = quant_s2(tok, qqE[qr], 0, w3)
            e1["prev_chain"] = qv
            e1["out_dma"][ji] = dma(
                "gpsimd", sSCR, raw[rt * 128:(rt + 1) * 128, cc:cc + F],
                qqE[qr].ap()[:, 0:F], waits=[(sDVE, qv)])

        def e1_pump():
            e1_do_stage2()
            e1_stage1()

        stAttnT = ExitStack()
        attnT = mksb(stAttnT, "attnT", [128, NP, S], BF16)
        stB = ExitStack()
        wstg = mksb(stB, "wstg", [128, KC, 384], F32, side=R)
        wqkvp = mksb(stB, "wqkvp", [128, KC, 384], BF16, side=R)
        q_tok = mksb(stB, "q_tok", [128, NT, 128], BF16, side=R)
        k_tok = mksb(stB, "k_tok", [128, NT, 128], BF16, side=R)
        vaug = mksb(stB, "vaug", [128, NT, 2, 65], BF16, side=R)
        tAb = mksb(stB, "tAb", [128, NT, 2, 2, 32], BF16, side=R)
        tBb = mksb(stB, "tBb", [128, NT, 2, 2, 32], BF16, side=R)
        QTr = mksb(stB, "QTr", [128, S], BF16, side=R)
        KTr = mksb(stB, "KTr", [128, S], BF16, side=R)
        PT = mksb(stB, "PT", [128, 4, 512], BF16, side=R)
        att65 = [mksb(stB, f"att65_{i}", [65, 512], BF16, side=R)
                 for i in range(2)]
        attok = mksb(stB, "attok", [128, NT, 65], BF16, side=R)
        attokn = mksb(stB, "attokn", [128, NT, 64], BF16, side=R)
        rd16 = mksb(stB, "rd16", [128, NT], F32, side=R)

        v_ones = sch.add("vector", (lambda v: v.memset(
            vaug.ap()[:, :, :, 64:65], 1.0)),
            waits=[(sDVE, v_consts)], inc=sDVE)

        sc_banks = [psC, psD]
        av_banks = [psO.ap()[0:65, 0:512], psO.ap()[0:65, 512:1024]]
        sc_last_exp = [0, 0]
        av_last_evac = [0, 0]
        sc_ctr = 0
        wcast_done = 0
        attok_prev = 0
        t2_prev_pe = 0
        for pr in range(NP):
            ws = ([(sDVE, wcast_done)] if wcast_done
                  else [(sACT, phaseA_act)])
            wdma = dma("sync", sDMA, wstg[:],
                       wqkv_h[:, pr * 384:(pr + 1) * 384]
                       .rearrange("(c p) n -> p c n", p=128), waits=ws)
            wcast_done = sch.add("vector", (lambda v: v.tensor_copy(
                wqkvp.ap().rearrange("p c n -> p (c n)"),
                wstg.ap().rearrange("p c n -> p (c n)"))),
                waits=[(sDMA, wdma)], inc=sDVE)

            evacB = {}
            for t in range(NT):
                ps = [psA, psB][t % 2]

                def qkv(te, t=t, ps=ps, pr=pr):
                    for kc in range(KC):
                        te.matmul(ps[:, 0:384],
                                  xlT.ap()[:, kc, t * 128:(t + 1) * 128],
                                  wqkvp.ap()[:, kc, :],
                                  start=(kc == 0), stop=False)
                    return te.matmul(ps[:, 0:384], ones1.ap(),
                                     bqkvs.ap()[:, pr * 384:(pr + 1) * 384],
                                     start=False, stop=True)
                waits = [(sDVE, wcast_done), (sACT, bias_cast_done),
                         (sACT, phaseA_act)]
                if t >= 2:
                    waits.append((sACT, evacB[t - 2]))
                elif pr > 0:
                    waits.append((sACT, sACT.n))
                pm = sch.add("tensor", qkv, waits=waits, inc=sPE)

                def ev(c, t=t, ps=ps):
                    c.copy(q_tok[:, t, :], ps[:, 0:128])
                    c.copy(k_tok[:, t, :], ps[:, 128:256])
                    return c.copy(
                        vaug.ap()[:, t, :, 0:64],
                        ps.ap()[:, 256:384]
                        .rearrange("p (a l) -> p a l", l=64))
                evacB[t] = sch.add("scalar", ev, waits=[(sPE, pm)], inc=sACT)
            qkv_evac = sACT.n
            e1_pump()

            def rope(v, src):
                sv = src.ap().rearrange("p n (h x j) -> p n h x j",
                                        h=2, x=2, j=32)
                cb = cosb.ap().unsqueeze(2).unsqueeze(3).broadcast_to(
                    [128, NT, 2, 1, 32])
                sbb = sinb.ap().unsqueeze(2).unsqueeze(3).broadcast_to(
                    [128, NT, 2, 1, 32])
                x0 = sv[:, :, :, 0:1, :]
                x1 = sv[:, :, :, 1:2, :]
                v.tensor_tensor(tAb.ap()[:, :, :, 0:1, :], x0, cb, Alu.mult)
                v.tensor_tensor(tBb.ap()[:, :, :, 0:1, :], x1, sbb, Alu.mult)
                v.tensor_tensor(tAb.ap()[:, :, :, 1:2, :], x0, sbb, Alu.mult)
                v.tensor_tensor(tBb.ap()[:, :, :, 1:2, :], x1, cb, Alu.mult)
                v.tensor_tensor(sv[:, :, :, 0:1, :], tAb.ap()[:, :, :, 0:1, :],
                                tBb.ap()[:, :, :, 0:1, :], Alu.subtract)
                return v.tensor_tensor(sv[:, :, :, 1:2, :],
                                       tAb.ap()[:, :, :, 1:2, :],
                                       tBb.ap()[:, :, :, 1:2, :], Alu.add)
            vr_q = sch.add("vector", (lambda v: rope(v, q_tok)),
                           waits=[(sACT, qkv_evac)], inc=sDVE)
            vr_k = sch.add("vector", (lambda v: rope(v, k_tok)),
                           waits=[(sACT, qkv_evac)], inc=sDVE)

            tr_evac = {}
            for which, src, dstT, vw in (("q", q_tok, QTr, vr_q),
                                         ("k", k_tok, KTr, vr_k)):
                for g in range(2):
                    ps = psTs[g % 2]

                    def trs(te, src=src, g=g, ps=ps):
                        ins = None
                        for jj in range(8):
                            t = g * 8 + jj
                            ins = te.transpose(ps[:, jj * 128:(jj + 1) * 128],
                                               src[:, t, :], idn[:])
                        return ins
                    waits = [(sDVE, vw)]
                    if (which, g) == ("q", 0):
                        waits.append((sACT, qkv_evac))
                        if pr == 0:
                            waits.append((sACT, evacA[NT - 1]))
                        else:
                            waits.append((sACT, attok_prev))
                    if tr_evac:
                        waits.append((sACT, max(tr_evac.values())))
                    ptr = sch.add("tensor", trs, waits=waits, inc=sPE)
                    tr_evac[(which, g)] = sch.add("scalar", (
                        lambda c, dstT=dstT, g=g, ps=ps: c.copy(
                            dstT.ap()[:, g * 1024:(g + 1) * 1024], ps[:])),
                        waits=[(sPE, ptr)], inc=sACT)
            trq_done = sACT.n

            for hh in range(2):
                po = 64 * hh
                for qb in range(4):
                    nkc = 4 * qb + 4
                    abk = (pr * 8 + hh * 4 + qb) % 2
                    exp_of = {}
                    av_of = {}
                    pend = []
                    for kc in range(nkc):
                        bk = sc_ctr % 2
                        sc_ctr += 1

                        def smm(te, kc=kc, qb=qb, po=po, ps=sc_banks[bk]):
                            return te.matmul(
                                ps[:],
                                KTr.ap()[po:po + 64, kc * 128:(kc + 1) * 128],
                                QTr.ap()[po:po + 64, qb * 512:(qb + 1) * 512],
                                start=True, stop=True)
                        waits = [(sACT, trq_done)]
                        if sc_last_exp[bk]:
                            waits.append((sACT, sc_last_exp[bk]))
                        pm = sch.add("tensor", smm, waits=waits, inc=sPE)
                        pend.append((kc, bk, pm))
                        if len(pend) >= 2:
                            kcp, bkp, pmp = pend.pop(0)
                            ew = [(sPE, pmp)]
                            if kcp >= 4:
                                ew.append((sPE, av_of[kcp - 4]))
                            exp_of[kcp] = sch.add("scalar", (
                                lambda c, kcp=kcp, ps=sc_banks[bkp]:
                                c.activation(
                                    PT[:, kcp % 4, :], ps[:], AFT.Exp,
                                    bias=nb12.ap(), scale=0.125)),
                                waits=ew, inc=sACT)
                            sc_last_exp[bkp] = exp_of[kcp]
                            mneed = kcp >= 4 * qb
                            vmk = None
                            if mneed:
                                vmk = sch.add("vector", (
                                    lambda v, kcp=kcp, qb=qb: v.tensor_tensor(
                                        PT[:, kcp % 4, :], PT[:, kcp % 4, :],
                                        mskb.ap()[:, kcp - 4 * qb, :],
                                        Alu.mult)),
                                    waits=[(sACT, exp_of[kcp])], inc=sDVE)

                            def avm(te, kcp=kcp, hh=hh, nkc=nkc, abk=abk):
                                return te.matmul(
                                    av_banks[abk], vaug.ap()[:, kcp, hh, :],
                                    PT[:, kcp % 4, :],
                                    start=(kcp == 0), stop=(kcp == nkc - 1),
                                    skip_group_check=True)
                            avw = ([(sDVE, vmk)] if mneed
                                   else [(sACT, exp_of[kcp])])
                            if kcp == 0:
                                avw.append((sDVE, v_ones))
                                if av_last_evac[abk]:
                                    avw.append((sACT, av_last_evac[abk]))
                            av_of[kcp] = sch.add("tensor", avm, waits=avw,
                                                 inc=sPE)
                    while pend:
                        kcp, bkp, pmp = pend.pop(0)
                        ew = [(sPE, pmp)]
                        if kcp >= 4:
                            ew.append((sPE, av_of[kcp - 4]))
                        exp_of[kcp] = sch.add("scalar", (
                            lambda c, kcp=kcp, ps=sc_banks[bkp]:
                            c.activation(
                                PT[:, kcp % 4, :], ps[:], AFT.Exp,
                                bias=nb12.ap(), scale=0.125)),
                            waits=ew, inc=sACT)
                        sc_last_exp[bkp] = exp_of[kcp]
                        mneed = kcp >= 4 * qb
                        vmk = None
                        if mneed:
                            vmk = sch.add("vector", (
                                lambda v, kcp=kcp, qb=qb: v.tensor_tensor(
                                    PT[:, kcp % 4, :], PT[:, kcp % 4, :],
                                    mskb.ap()[:, kcp - 4 * qb, :], Alu.mult)),
                                waits=[(sACT, exp_of[kcp])], inc=sDVE)

                        def avm(te, kcp=kcp, hh=hh, nkc=nkc, abk=abk):
                            return te.matmul(
                                av_banks[abk], vaug.ap()[:, kcp, hh, :],
                                PT[:, kcp % 4, :],
                                start=(kcp == 0), stop=(kcp == nkc - 1),
                                skip_group_check=True)
                        avw = ([(sDVE, vmk)] if mneed
                               else [(sACT, exp_of[kcp])])
                        if kcp == 0:
                            avw.append((sDVE, v_ones))
                            if av_last_evac[abk]:
                                avw.append((sACT, av_last_evac[abk]))
                        av_of[kcp] = sch.add("tensor", avm, waits=avw,
                                             inc=sPE)
                    a65b = att65[abk]
                    av_last_evac[abk] = sch.add("scalar", (
                        lambda c, a65b=a65b, abk=abk: c.copy(
                            a65b[:], av_banks[abk])),
                        waits=[(sPE, av_of[nkc - 1])], inc=sACT)

                    def t1s(te, a65b=a65b):
                        ins = None
                        for qc in range(4):
                            ins = te.transpose(
                                psT1[:, qc * 128:qc * 128 + 65],
                                a65b.ap()[:, qc * 128:(qc + 1) * 128],
                                idn.ap()[0:65, 0:65])
                        return ins
                    wt1 = [(sACT, av_last_evac[abk])]
                    if attok_prev:
                        wt1.append((sACT, attok_prev))
                    pt1 = sch.add("tensor", t1s, waits=wt1, inc=sPE)
                    attok_prev = sch.add("scalar", (lambda c, qb=qb: c.copy(
                        attok.ap()[:, qb * 4:(qb + 1) * 4, :],
                        psT1.ap()[:, 0:512]
                        .rearrange("p (qc f) -> p qc f", f=128)
                        [:, :, 0:65])),
                        waits=[(sPE, pt1)], inc=sACT)
                    if qb >= 1:
                        e1_pump()
                all_attok = sACT.n
                vrd = sch.add("vector", (lambda v: v.reciprocal(
                    rd16[:], attok.ap()[:, :, 64])),
                    waits=[(sACT, all_attok), (sPE, t2_prev_pe)], inc=sDVE)
                vnorm = sch.add("vector", (lambda v: v.tensor_tensor(
                    attokn[:], attok.ap()[:, :, 0:64],
                    rd16.ap().unsqueeze(2).broadcast_to([128, NT, 64]),
                    Alu.mult)), waits=[(sDVE, vrd)], inc=sDVE)
                for g in range(2):
                    def t2s(te, g=g):
                        ins = None
                        for jj in range(8):
                            qt = g * 8 + jj
                            ins = te.transpose(
                                psT2[0:64, jj * 128:(jj + 1) * 128],
                                attokn[:, qt, :], idn[:])
                        return ins
                    wt2 = [(sDVE, vnorm), (sACT, sACT.n)]
                    t2_prev_pe = sch.add("tensor", t2s, waits=wt2, inc=sPE)
                    sch.add("scalar", (lambda c, g=g, po=po, pr=pr: c.copy(
                        attnT.ap()[po:po + 64, pr, g * 1024:(g + 1) * 1024]
                        .rearrange("p (jj f) -> p jj f", f=128),
                        psT2.ap()[0:64, :]
                        .rearrange("p (jj f) -> p jj f", f=128))),
                        waits=[(sPE, t2_prev_pe)], inc=sACT)
                e1_pump()
        stB_attn_done = sACT.n
        stB_dve_done = sDVE.n
        dbg("d_attnT", [128, NP, S], BF16, attnT[:], [(sACT, stB_attn_done)])
        sch.run(block)
        stB.close()

        stC = ExitStack()
        wostg = mksb(stC, "wostg", [128, KC, 512], F32, side=R)
        wos = mksb(stC, "wos", [128, KC, D], BF16, side=R)
        x_c = [mksb(stC, f"x_c{i}", [128, D], F32, side=R) for i in range(2)]
        x1b = [mksb(stC, f"x1b{i}", [128, D], F32, side=R) for i in range(2)]
        sqt2 = mksb(stC, "sqt2", [128, D], F32, side=R)
        mv2 = [mksb(stC, f"mv2_{i}", [128, 6], F32, side=R) for i in range(2)]
        rt2 = [mksb(stC, f"rt2_{i}", [128, 2], F32, side=R) for i in range(2)]

        for nh in range(2):
            wdma = dma("sync", sDMA, wostg[:],
                       wo_h[:, nh * 512:(nh + 1) * 512]
                       .rearrange("(c p) n -> p c n", p=128),
                       waits=([(sDVE, sDVE.n)] if nh
                              else [(sACT, stB_attn_done),
                                    (sDVE, stB_dve_done)]))
            sch.add("vector", (lambda v, nh=nh: v.tensor_copy(
                wos.ap()[:, :, nh * 512:(nh + 1) * 512], wostg.ap())),
                waits=[(sDMA, wdma)], inc=sDVE)
        wo_cast = sDVE.n

        epiC = {}
        dumpC = {}
        aC = {}
        vC = {}
        for t in range(NT):
            r = t % 2
            wx = [(sACT, stB_attn_done)]
            if t >= 2:
                wx += [(sDVE, vC[t - 2]), (sACT, aC[t - 2])]
            wxd = dma("sync", sDMA, x_c[r][:], x_v[:, t, :], waits=wx)

            def omm(te, t=t):
                ins = None
                for nh in range(2):
                    for pr in range(NP):
                        te.matmul(psO[:, nh * 512:(nh + 1) * 512],
                                  attnT.ap()[:, pr, t * 128:(t + 1) * 128],
                                  wos.ap()[:, pr, nh * 512:(nh + 1) * 512],
                                  start=(pr == 0), stop=False)
                    ins = te.matmul(psO[:, nh * 512:(nh + 1) * 512],
                                    ones1.ap(),
                                    bos.ap()[:, nh * 512:(nh + 1) * 512],
                                    start=False, stop=True)
                return ins
            waits = [(sACT, stB_attn_done), (sDVE, wo_cast)]
            if t >= 1:
                waits.append((sDVE, epiC[t - 1]))
            pm = sch.add("tensor", omm, waits=waits, inc=sPE)
            epw = [(sPE, pm), (sDMA, wxd)]
            if t >= 2:
                epw.append((sSCR, dumpC[t - 2]))
            epiC[t] = sch.add("vector", (lambda v, t=t, r=r: v.tensor_tensor(
                x1b[r][:], x_c[r][:], psO[:], Alu.add)), waits=epw, inc=sDVE)
            dumpC[t] = dma("gpsimd", sSCR, xscr_v[:, t, :], x1b[r][:],
                           waits=[(sDVE, epiC[t])])
            vsum = sch.add("vector", (lambda v, r=r: v.reduce_sum(
                mv2[r][:, 0:1], x1b[r][:], AX.X)),
                waits=[(sDVE, epiC[t])], inc=sDVE)
            asq = sch.add("scalar", (lambda c, r=r: c.activation(
                sqt2[:], x1b[r][:], AFT.Square, accum_out=mv2[r][:, 1:2])),
                waits=[(sDVE, epiC[t])], inc=sACT)

            def stat2d(v, t=t, r=r):
                v.tensor_scalar(mv2[r][:, 2:3], mv2[r][:, 0:1], 1.0 / D, None,
                                Alu.mult)
                v.tensor_scalar(negm_all[:, t:t + 1], mv2[r][:, 0:1],
                                -1.0 / D, None, Alu.mult)
                return v.tensor_scalar(mv2[r][:, 3:4], mv2[r][:, 1:2],
                                       1.0 / D, 1e-5, Alu.mult, Alu.add)
            v2a = sch.add("vector", stat2d, waits=[(sACT, asq), (sDVE, vsum)],
                          inc=sDVE)
            v2b = sch.add("vector", (lambda v, r=r: v.tensor_tensor(
                mv2[r][:, 4:5], mv2[r][:, 2:3], mv2[r][:, 2:3], Alu.mult)),
                waits=[(sDVE, v2a)], inc=sDVE)
            vst = sch.add("vector", (lambda v, r=r: v.tensor_tensor(
                rt2[r][:, 0:1], mv2[r][:, 3:4], mv2[r][:, 4:5],
                Alu.subtract)), waits=[(sDVE, v2b)], inc=sDVE)
            vC[t] = sch.add("vector", (lambda v, r=r: v.reciprocal(
                rt2[r][:, 1:2], rt2[r][:, 0:1])), waits=[(sDVE, vst)],
                inc=sDVE)
            aC[t] = sch.add("scalar", (lambda c, t=t, r=r: c.activation(
                rstd_all[:, t:t + 1], rt2[r][:, 1:2], AFT.Sqrt)),
                waits=[(sDVE, vC[t])], inc=sACT)
            e1_pump()
            e1_pump()
        while e1["ji"] < len(wjobs) or e1["stage2"] is not None:
            e1_pump()
        e1_done_scr = sSCR.n
        e1_done_dve = sDVE.n
        ln2_done_act = sACT.n
        ln2_done_vec = sDVE.n
        xdump_done = sSCR.n
        dbg("d_rstd", [128, NT], F32, rstd_all[:], [(sACT, ln2_done_act)])
        dbg("d_negm", [128, NT], F32, negm_all[:], [(sDVE, ln2_done_vec)])
        sch.run(block)
        stC.close()
        stEq.close()
        stXlT.close()
        stAttnT.close()
        stCst.close()

        stE = ExitStack()
        wgqT = mksb(stE, "wgqT", [128, KC, DFF], BF16)
        wuqT = mksb(stE, "wuqT", [128, KC, DFF], BF16)
        wdqT = mksb(stE, "wdqT", [128, NJ, D], BF16)
        stEc = ExitStack()
        ebuf = [mksb(stEc, f"ebuf{i}", [128, DFF], BF16, side=R)
                for i in range(2)]

        tjobs = []
        for rt in range(NJ):
            tjobs.append((wgq_raw, wgqT, "gu", rt, 1024))
        for rt in range(NJ):
            tjobs.append((wuq_raw, wuqT, "gu", rt, 1024))
        for rt in range(KC):
            tjobs.append((wdq_raw, wdqT, "d", rt, DFF))

        prev_evacE = 0
        prev_ldE = {}
        for ji, (raw, dstT, kind, rt, F) in enumerate(tjobs):
            r = ji % 2
            ws = [(sDVE, ln2_done_vec), (sACT, ln2_done_act),
                  (sSCR, e1_done_scr)] if ji < 2 else [(sACT, prev_ldE[ji - 2])]
            wdma = dma("sync", sDMA, ebuf[r][:, 0:F],
                       raw[rt * 128:(rt + 1) * 128, 0:F], waits=ws)
            njj = F // 128
            for half in range((njj + 7) // 8):
                j0 = half * 8
                jn = min(8, njj - j0)
                pst = psTs[half % 2]

                def trs(te, r=r, j0=j0, jn=jn, pst=pst):
                    ins = None
                    for jj in range(jn):
                        ins = te.transpose(
                            pst[:, jj * 128:(jj + 1) * 128],
                            ebuf[r][:, (j0 + jj) * 128:(j0 + jj + 1) * 128],
                            idn[:])
                    return ins
                wtr = [(sDMA, wdma)]
                if prev_evacE:
                    wtr.append((sACT, prev_evacE))
                ptr = sch.add("tensor", trs, waits=wtr, inc=sPE)
                if kind == "d":
                    def ev(c, rt=rt, j0=j0, jn=jn, pst=pst):
                        return c.copy(
                            wdqT.ap()[:, j0:j0 + jn,
                                      rt * 128:(rt + 1) * 128],
                            pst.ap()[:, 0:jn * 128]
                            .rearrange("p (jj f) -> p jj f", f=128))
                else:
                    def ev(c, dstT=dstT, rt=rt, j0=j0, jn=jn, pst=pst):
                        return c.copy(
                            dstT.ap()[:, j0:j0 + jn,
                                      rt * 128:(rt + 1) * 128],
                            pst.ap()[:, 0:jn * 128]
                            .rearrange("p (jj f) -> p jj f", f=128))
                prev_evacE = sch.add("scalar", ev, waits=[(sPE, ptr)],
                                     inc=sACT)
            prev_ldE[ji] = prev_evacE
        wq_done_act = sACT.n
        wq_done_vec = sDVE.n
        wq_done_pe = sPE.n
        dbg("d_wgqT", [128, KC, DFF], BF16, wgqT[:], [(sACT, wq_done_act)])
        dbg("d_wdqT", [128, NJ, D], BF16, wdqT[:], [(sACT, wq_done_act)])
        sch.run(block)
        stEc.close()

        stF = ExitStack()
        x_t = [mksb(stF, f"x_t{i}", [128, D], F32, side=R) for i in range(2)]
        xl2c = mksb(stF, "xl2c", [128, D], F32, side=R)
        xqb = mksb(stF, "xqb", [128, D], BF16, side=R)
        xqT_t = mksb(stF, "xqT_t", [128, KC, 128], BF16, side=R)
        silu_sb = [mksb(stF, f"silu{i}", [128, 512], F32, side=R)
                   for i in range(2)]
        up_sb = [mksb(stF, f"up{i}", [128, 512], F32, side=R)
                 for i in range(2)]
        hmat = [mksb(stF, f"hmat{i}", [128, 512], F32, side=R)
                for i in range(2)]
        hqb = [mksb(stF, f"hqb{i}", [128, 512], BF16, side=R)
               for i in range(2)]
        hqT_t = mksb(stF, "hqT_t", [128, NJ, 128], BF16, side=R)
        outb = [mksb(stF, f"outb{i}", [128, D], F32, side=R) for i in range(2)]
        FQ2 = 512
        bufsF = []
        for i in range(2):
            bufsF.append((
                mksb(stF, f"quF{i}", [128, FQ2], F32, side=R),
                mksb(stF, f"qtF{i}", [128, FQ2], F32, side=R),
                mksb(stF, f"qcAF{i}", [128, FQ2], F32, side=R),
                mksb(stF, f"qcBF{i}", [128, FQ2], F32, side=R),
                mksb(stF, f"qcCF{i}", [128, FQ2], F32, side=R),
                mksb(stF, f"qsgF{i}", [128, FQ2], BF16, side=R),
                mksb(stF, f"qs2F{i}", [128, FQ2], BF16, side=R),
                mksb(stF, f"qbmF{i}", [128, FQ2 // 16], F32, side=R),
                mksb(stF, f"qshF{i}", [128, FQ2 // 16], F32, side=R),
                mksb(stF, f"qrbF{i}", [128, FQ2 // 16], F32, side=R),
                mksb(stF, f"qbcF{i}", [128, FQ2 // 16], F32, side=R),
                mksb(stF, f"qr2F{i}", [128, FQ2 // 16], F32, side=R),
                mksb(stF, f"qxsF{i}", [128, FQ2], F32, side=R),
            ))

        ncl = [(0, 512), (512, 512), (1024, 512), (1536, 512), (2048, 512),
               (2560, 256)]
        g_banks = [psA, psC]
        u_banks = [psB, psD]
        g_last = [0, 0]
        u_last = [0, 0]
        chain_last_tr = {}
        prev_down = 0
        prev_epi = 0
        out_dma = {}
        gi = 0
        for t in range(NT):
            r = t % 2
            wx = [(sSCR, xdump_done)]
            if t < 2:
                wx += [(sDVE, wq_done_vec), (sACT, wq_done_act),
                       (sPE, wq_done_pe)]
            if t >= 2:
                wx.append((sDVE, prev_epi))
            wxd = dma("sync", sDMA, x_t[r][:], xscr_v[:, t, :], waits=wx)
            ac = sch.add("scalar", (lambda c, t=t, r=r: c.activation(
                xl2c[:], x_t[r][:], AFT.Identity,
                bias=negm_all.ap()[:, t:t + 1], scale=1.0)),
                waits=[(sDMA, wxd), (sACT, wq_done_act), (sACT, ln2_done_act)],
                inc=sACT)
            w30 = [(sPE, chain_last_tr[0])] if 0 in chain_last_tr else []
            q1 = quant_chain(bufsF[0], 512, xl2c, 0, xqb, 0,
                             [(sACT, ac)], w30)
            w31 = [(sPE, chain_last_tr[1])] if 1 in chain_last_tr else []
            q2 = quant_chain(bufsF[1], 512, xl2c, 512, xqb, 512, [], w31)

            def trx(te):
                ins = None
                for jj in range(KC):
                    ins = te.transpose(psT1[:, jj * 128:(jj + 1) * 128],
                                       xqb[:, jj * 128:(jj + 1) * 128], idn[:])
                return ins
            wtx = [(sDVE, q2), (sACT, sACT.n)]
            ptx = sch.add("tensor", trx, waits=wtx, inc=sPE)
            chain_last_tr[0] = ptx
            chain_last_tr[1] = ptx
            exq = sch.add("scalar", (lambda c: c.copy(
                xqT_t.ap().rearrange("p c f -> p (c f)"), psT1[:])),
                waits=[(sPE, ptx)], inc=sACT)

            pendF = []

            def plan_tail(ncc, nsz, cset):
                njj = nsz // 128
                pst = psTs[cset]
                tok = pend_qh[(ncc, cset)]
                w3h = ([(sPE, chain_last_tr[cset])]
                       if cset in chain_last_tr else [])
                qh = quant_s2(tok, hqb[cset], 0, w3h)

                def trh(te, njj=njj, cset=cset, pst=pst):
                    ins = None
                    for jj in range(njj):
                        ins = te.transpose(
                            pst[:, jj * 128:(jj + 1) * 128],
                            hqb[cset][:, jj * 128:(jj + 1) * 128], idn[:])
                    return ins
                wth = [(sDVE, qh), (sACT, sACT.n)]
                pth = sch.add("tensor", trh, waits=wth, inc=sPE)
                chain_last_tr[cset] = pth
                ehq = sch.add("scalar", (lambda c, ncc=ncc, njj=njj, pst=pst:
                              c.copy(
                    hqT_t.ap()[:, ncc // 128:ncc // 128 + njj, :],
                    pst.ap()[:, 0:njj * 128]
                    .rearrange("p (jj f) -> p jj f", f=128))),
                    waits=[(sPE, pth)], inc=sACT)

                def dmm(te, ncc=ncc, njj=njj):
                    ins = None
                    for jj in range(njj):
                        j = ncc // 128 + jj
                        for nh in range(2):
                            ins = te.matmul(
                                psO[:, nh * 512:(nh + 1) * 512],
                                hqT_t.ap()[:, j, :],
                                wdqT.ap()[:, j, nh * 512:(nh + 1) * 512],
                                start=(j == 0), stop=(j == NJ - 1),
                                skip_group_check=True)
                    return ins
                wd_ = [(sACT, ehq)]
                if ncc == 0 and t >= 1:
                    wd_.append((sDVE, prev_epi))
                return sch.add("tensor", dmm, waits=wd_, inc=sPE)

            pend_qh = {}
            for nci, (ncc, nsz) in enumerate(ncl):
                cset = gi % 2
                gb = g_banks[cset]
                ub = u_banks[cset]
                gi += 1

                def gmm(te, ncc=ncc, nsz=nsz, gb=gb):
                    ins = None
                    for kc in range(KC):
                        ins = te.matmul(gb[:, 0:nsz], xqT_t.ap()[:, kc, :],
                                        wgqT.ap()[:, kc, ncc:ncc + nsz],
                                        start=(kc == 0), stop=(kc == KC - 1))
                    return ins

                def umm(te, ncc=ncc, nsz=nsz, ub=ub):
                    ins = None
                    for kc in range(KC):
                        ins = te.matmul(ub[:, 0:nsz], xqT_t.ap()[:, kc, :],
                                        wuqT.ap()[:, kc, ncc:ncc + nsz],
                                        start=(kc == 0), stop=(kc == KC - 1))
                    return ins
                wg_ = [(sACT, exq)]
                if g_last[cset]:
                    wg_.append((sACT, g_last[cset]))
                pg = sch.add("tensor", gmm, waits=wg_, inc=sPE)
                wu_ = [(sACT, u_last[cset])] if u_last[cset] else []
                pu = sch.add("tensor", umm, waits=wu_, inc=sPE)
                asl = sch.add("scalar", (lambda c, t=t, nsz=nsz, gb=gb,
                                         cset=cset: c.activation(
                    silu_sb[cset][:, 0:nsz], gb[:, 0:nsz], AFT.Silu,
                    scale=rstd_all.ap()[:, t:t + 1])),
                    waits=[(sPE, pg)], inc=sACT)
                g_last[cset] = asl
                aup = sch.add("scalar", (lambda c, t=t, nsz=nsz, ub=ub,
                                         cset=cset: c.activation(
                    up_sb[cset][:, 0:nsz], ub[:, 0:nsz], AFT.Identity,
                    scale=rstd_all.ap()[:, t:t + 1])),
                    waits=[(sPE, pu)], inc=sACT)
                u_last[cset] = aup
                vh = sch.add("vector", (lambda v, nsz=nsz, cset=cset:
                             v.tensor_tensor(
                    hmat[cset][:, 0:nsz], silu_sb[cset][:, 0:nsz],
                    up_sb[cset][:, 0:nsz], Alu.mult)),
                    waits=[(sACT, asl), (sACT, aup)], inc=sDVE)
                tok = quant_s1(bufsF[cset], nsz, hmat[cset], 0,
                               [(sDVE, vh)])
                pend_qh[(ncc, cset)] = tok
                if pendF:
                    prev_down = plan_tail(*pendF.pop(0))
                pendF.append((ncc, nsz, cset))
            while pendF:
                prev_down = plan_tail(*pendF.pop(0))
            epw = [(sPE, prev_down)]
            if t >= 2:
                epw.append((sSCR, out_dma[t - 2]))
            prev_epi = sch.add("vector", (lambda v, r=r: v.tensor_tensor(
                outb[r][:], x_t[r][:], psO[:], Alu.add)), waits=epw, inc=sDVE)
            out_dma[t] = dma("gpsimd", sSCR, out_v[:, t, :], outb[r][:],
                             waits=[(sDVE, prev_epi)])
        final_out = sSCR.n

        sch.add("sync", (lambda c: c.wait_ge(sSCR.h, final_out)), waits=[])
        sch.run(block)
        stF.close()
        stE.close()
        nc._dbg_outs = dbg_outs
    return nc


_NC_CACHE = {}


def _get_nc():
    if "nc" not in _NC_CACHE:
        _NC_CACHE["nc"] = build_nc()
    return _NC_CACHE["nc"]


def _rope_tables():
    inv = 1.0 / 10000.0 ** (2.0 * np.arange(32, dtype=np.float64) / 64)
    pos = np.arange(S, dtype=np.float64)
    ang = pos[:, None] * inv[None, :]
    return (np.cos(ang).astype(np.float32), np.sin(ang).astype(np.float32))


def _perm():
    p = np.zeros(D, dtype=np.int64)
    for h in range(H):
        for j in range(32):
            p[h * 64 + j] = h * 64 + 2 * j
            p[h * 64 + 32 + j] = h * 64 + 2 * j + 1
    return p


def _masks():
    out = np.zeros((128, 4, 512), dtype=np.float32)
    kt = np.arange(128)[:, None]
    qt = np.arange(512)[None, :]
    for m in range(4):
        out[:, m, :] = (qt >= m * 128 + kt).astype(np.float32)
    return out.reshape(128, 4 * 512)


def kernel(L0_attn_x, L0_attn_Wq, L0_attn_bq, L0_attn_Wk, L0_attn_bk,
           L0_attn_Wv, L0_attn_bv, L0_attn_Wo, L0_attn_bo,
           L0_ffn_w_gate, L0_ffn_w_up, L0_ffn_w_down):
    x = np.ascontiguousarray(np.asarray(L0_attn_x, dtype=np.float32))
    perm = _perm()
    wq = np.asarray(L0_attn_Wq, dtype=np.float32)[:, perm]
    wk = np.asarray(L0_attn_Wk, dtype=np.float32)[:, perm]
    bq = np.asarray(L0_attn_bq, dtype=np.float32)[perm]
    bk = np.asarray(L0_attn_bk, dtype=np.float32)[perm]
    wv = np.asarray(L0_attn_Wv, dtype=np.float32)
    bv = np.asarray(L0_attn_bv, dtype=np.float32)
    wqkv = np.empty((D, NP * 384), dtype=np.float32)
    bqkv = np.empty((1, NP * 384), dtype=np.float32)
    for pr in range(NP):
        cs = slice(pr * 128, (pr + 1) * 128)
        wqkv[:, pr * 384 + 0:pr * 384 + 128] = wq[:, cs]
        wqkv[:, pr * 384 + 128:pr * 384 + 256] = wk[:, cs]
        wqkv[:, pr * 384 + 256:pr * 384 + 384] = wv[:, cs]
        bqkv[0, pr * 384 + 0:pr * 384 + 128] = bq[cs]
        bqkv[0, pr * 384 + 128:pr * 384 + 256] = bk[cs]
        bqkv[0, pr * 384 + 256:pr * 384 + 384] = bv[cs]
    wo = np.ascontiguousarray(np.asarray(L0_attn_Wo, dtype=np.float32))
    bo = np.asarray(L0_attn_bo, dtype=np.float32).reshape(1, D).copy()
    wg = np.ascontiguousarray(np.asarray(L0_ffn_w_gate, dtype=np.float32))
    wu = np.ascontiguousarray(np.asarray(L0_ffn_w_up, dtype=np.float32))
    wd = np.ascontiguousarray(np.asarray(L0_ffn_w_down, dtype=np.float32))
    cosv, sinv = _rope_tables()
    idn = np.eye(128, dtype=np.float32)
    msk = _masks()

    nc = _get_nc()
    in_maps = []
    for b in range(8):
        in_maps.append({
            "x": np.ascontiguousarray(x[b]),
            "wqkv": wqkv, "wo": wo, "bqkv": bqkv, "bo": bo,
            "wg": wg, "wu": wu, "wd": wd,
            "cosv": cosv, "sinv": sinv, "idn": idn, "msk": msk,
        })
    res = run_bass_kernel_spmd(nc, in_maps, core_ids=list(range(8)))
    return np.stack([np.asarray(r["out"], dtype=np.float32)
                     for r in res.results], axis=0)


# version 10
# speedup vs baseline: 1.0754x; 1.0754x over previous
import numpy as np
import ml_dtypes
import concourse.bass as bass
import concourse.mybir as mybir
from concourse.bass_utils import run_bass_kernel_spmd
from contextlib import ExitStack

F32 = mybir.dt.float32
BF16 = mybir.dt.bfloat16
Alu = mybir.AluOpType
AFT = mybir.ActivationFunctionType
AX = mybir.AxisListType

S, D, H = 2048, 1024, 16
NT = S // 128
KC = D // 128
NP = H // 2
DFF = 2816
NJ = DFF // 128

_MIDS = [0.25, 0.75, 1.25, 1.75, 2.5, 3.5, 5.0]
TH = [float(np.float32((m / 6.0) ** 2)) for m in _MIDS]


class Sem:
    def __init__(self, handle):
        self.h = handle
        self.n = 0


class Sched:
    ENGS = ("sync", "tensor", "scalar", "vector", "gpsimd")

    def __init__(self):
        self.steps = {e: [] for e in self.ENGS}

    def add(self, eng, emit, waits=(), inc=None, inc_amt=1):
        self.steps[eng].append((list(waits), emit, inc, inc_amt))
        if inc is not None:
            inc.n += inc_amt
            return inc.n
        return None

    def run(self, block):
        for eng in self.ENGS:
            steps = self.steps[eng]
            if not steps:
                continue

            def mk(steps):
                def f(ctx):
                    for waits, emit, inc, amt in steps:
                        for sem, val in waits:
                            ctx.wait_ge(sem.h, val)
                        ins = emit(ctx)
                        if inc is not None:
                            ins.then_inc(inc.h, amt)
                return f
            getattr(block, eng)(mk(steps))
        self.steps = {e: [] for e in self.ENGS}


def build_nc(debug=False):
    nc = bass.Bass("TRN2", target_bir_lowering=False)

    x_h = nc.dram_tensor("x", [S, D], F32, kind="ExternalInput").ap()
    wqkv_h = nc.dram_tensor("wqkv", [D, NP * 384], F32,
                            kind="ExternalInput").ap()
    wo_h = nc.dram_tensor("wo", [D, D], F32, kind="ExternalInput").ap()
    bqkv_h = nc.dram_tensor("bqkv", [1, NP * 384], F32,
                            kind="ExternalInput").ap()
    bo_h = nc.dram_tensor("bo", [1, D], F32, kind="ExternalInput").ap()
    wg_h = nc.dram_tensor("wg", [DFF, D], F32, kind="ExternalInput").ap()
    wu_h = nc.dram_tensor("wu", [DFF, D], F32, kind="ExternalInput").ap()
    wd_h = nc.dram_tensor("wd", [D, DFF], F32, kind="ExternalInput").ap()
    cos_h = nc.dram_tensor("cosv", [S, 32], F32, kind="ExternalInput").ap()
    sin_h = nc.dram_tensor("sinv", [S, 32], F32, kind="ExternalInput").ap()
    idn_h = nc.dram_tensor("idn", [128, 128], F32, kind="ExternalInput").ap()
    msk_h = nc.dram_tensor("msk", [128, 4 * 512], F32, kind="ExternalInput").ap()
    out_h = nc.dram_tensor("out", [S, D], F32, kind="ExternalOutput").ap()
    xscr = nc.dram_tensor("xscr", [S, D], F32, kind="Internal").ap()
    wgq_raw = nc.dram_tensor("wgq_raw", [DFF, D], BF16, kind="Internal").ap()
    wuq_raw = nc.dram_tensor("wuq_raw", [DFF, D], BF16, kind="Internal").ap()
    wdq_raw = nc.dram_tensor("wdq_raw", [D, DFF], BF16, kind="Internal").ap()
    x_v = x_h.rearrange("(n p) d -> p n d", p=128)
    xscr_v = xscr.rearrange("(n p) d -> p n d", p=128)
    out_v = out_h.rearrange("(n p) d -> p n d", p=128)

    root = ExitStack()

    def mksb(st, name, shape, dtype=F32, side=None):
        kw = {"side": side} if side else {}
        return st.enter_context(nc.sbuf_tensor(name, shape, dtype, **kw))

    R = "right"

    sDMA = Sem(root.enter_context(nc.semaphore("sDMA")))
    sSCR = Sem(root.enter_context(nc.semaphore("sSCR")))
    sPE = Sem(root.enter_context(nc.semaphore("sPE")))
    sACT = Sem(root.enter_context(nc.semaphore("sACT")))
    sDVE = Sem(root.enter_context(nc.semaphore("sDVE")))
    sPOOL = Sem(root.enter_context(nc.semaphore("sPOOL")))

    psA = root.enter_context(nc.psum_tensor("psA", [128, 512], F32))
    psB = root.enter_context(nc.psum_tensor("psB", [128, 512], F32))
    psC = root.enter_context(nc.psum_tensor("psC", [128, 512], F32))
    psD = root.enter_context(nc.psum_tensor("psD", [128, 512], F32))
    psO = root.enter_context(nc.psum_tensor("psO", [128, 1024], F32))
    psT1 = root.enter_context(nc.psum_tensor("psT1", [128, 1024], BF16))
    psT2 = root.enter_context(nc.psum_tensor("psT2", [128, 1024], BF16))
    psTs = [psT1, psT2]

    idn = mksb(root, "idn_b", [128, 128], BF16)
    ones1 = mksb(root, "ones1", [1, 128], BF16)
    epsc = mksb(root, "epsc", [128, 1], F32)
    nb12 = mksb(root, "nb12", [128, 1], F32)
    rstd_all = mksb(root, "rstd_all", [128, NT], F32)
    negm_all = mksb(root, "negm_all", [128, NT], F32)

    sch = Sched()

    def dma(eng, sem, out, in_, waits=()):
        def emit(ctx, out=out, in_=in_):
            return ctx.dma_start(out=out, in_=in_)
        return sch.add(eng, emit, waits=waits, inc=sem, inc_amt=16)

    dbg_outs = []

    def dbg(name, shape, dtype, src_ap, waits):
        if not debug:
            return
        hap = nc.dram_tensor(name, shape, dtype, kind="ExternalOutput").ap()
        dbg_outs.append(name)
        dma("gpsimd", sSCR, hap, src_ap, waits=list(waits))

    def quant_s1(bufs, F, src, soff, waits1):
        (qu, qt_, qa, qb_, qc_, sg, s2, bm, sh, rb, bmc, rb2, xsq) = bufs
        nb = F // 16

        def c1r(v):
            sv = src.ap()[:, soff:soff + F].rearrange(
                "p (b k) -> p b k", k=16)
            return v.tensor_reduce(bm[:, 0:nb], sv, AX.X, Alu.max,
                                   apply_absolute_value=True)
        v1r = sch.add("vector", c1r, waits=list(waits1), inc=sDVE)
        aq = sch.add("scalar", (lambda c: c.activation(
            xsq[:, 0:F], src.ap()[:, soff:soff + F], AFT.Square)),
            waits=list(waits1), inc=sACT)

        def c1(v):
            v.tensor_scalar(bmc[:, 0:nb], bm[:, 0:nb], 6e-12, None, Alu.max)
            v.tensor_scalar(sh[:, 0:nb], bm[:, 0:nb], 6e-12, 1.0 / 12.0,
                            Alu.max, Alu.mult)
            return v.tensor_scalar(sg[:, 0:F], src.ap()[:, soff:soff + F],
                                   0.0, None, Alu.is_ge)
        v1a = sch.add("vector", c1, waits=[(sDVE, v1r)], inc=sDVE)
        v1 = sch.add("vector", (lambda v: v.reciprocal(
            rb[:, 0:nb], bmc[:, 0:nb])), waits=[(sDVE, v1a)], inc=sDVE)
        v1b = sch.add("vector", (lambda v: v.tensor_tensor(
            rb2[:, 0:nb], rb[:, 0:nb], rb[:, 0:nb], Alu.mult)),
            waits=[(sDVE, v1)], inc=sDVE)
        return (bufs, F, v1b, aq)

    def quant_s2(tok, dst, doff, waits3):
        (bufs, F, v1b, aq) = tok
        (qu, qt_, qa, qb_, qc_, sg, s2, bm, sh, rb, bmc, rb2, xsq) = bufs
        nb = F // 16

        def c2(v):
            rbb = rb2.ap()[:, 0:nb].unsqueeze(2).broadcast_to([128, nb, 16])
            return v.tensor_tensor(
                qu.ap()[:, 0:F].rearrange("p (b k) -> p b k", k=16),
                xsq.ap()[:, 0:F].rearrange("p (b k) -> p b k", k=16),
                rbb, Alu.mult)
        v2 = sch.add("vector", c2, waits=[(sDVE, v1b), (sACT, aq)], inc=sDVE)

        def c3(v):
            u = qu.ap()[:, 0:F]
            v.tensor_scalar(qa[:, 0:F], u, TH[0], None, Alu.is_gt)
            v.scalar_tensor_tensor(qb_[:, 0:F], u, TH[1], qa.ap()[:, 0:F],
                                   Alu.is_gt, Alu.add)
            v.scalar_tensor_tensor(qa[:, 0:F], u, TH[2], qb_.ap()[:, 0:F],
                                   Alu.is_gt, Alu.add)
            v.scalar_tensor_tensor(qb_[:, 0:F], u, TH[3], qa.ap()[:, 0:F],
                                   Alu.is_gt, Alu.add)
            v.tensor_scalar(qc_[:, 0:F], u, TH[4], None, Alu.is_gt)
            v.scalar_tensor_tensor(qc_[:, 0:F], u, TH[5], qc_.ap()[:, 0:F],
                                   Alu.is_gt, Alu.add)
            v.scalar_tensor_tensor(qc_[:, 0:F], u, TH[6], qc_.ap()[:, 0:F],
                                   Alu.is_gt, Alu.add)
            v.scalar_tensor_tensor(qc_[:, 0:F], u, TH[6], qc_.ap()[:, 0:F],
                                   Alu.is_gt, Alu.add)
            v.scalar_tensor_tensor(qu[:, 0:F], qc_.ap()[:, 0:F], 2.0,
                                   qb_.ap()[:, 0:F], Alu.mult, Alu.add)
            shb = sh.ap()[:, 0:nb].unsqueeze(2).broadcast_to([128, nb, 16])
            v.tensor_tensor(
                qt_.ap()[:, 0:F].rearrange("p (b k) -> p b k", k=16),
                qu.ap()[:, 0:F].rearrange("p (b k) -> p b k", k=16),
                shb, Alu.mult)
            v.tensor_scalar(s2[:, 0:F], sg[:, 0:F], 2.0, -1.0,
                            Alu.mult, Alu.add)
            return v.tensor_tensor(dst.ap()[:, doff:doff + F],
                                   qt_.ap()[:, 0:F],
                                   s2.ap()[:, 0:F], Alu.mult)
        return sch.add("vector", c3, waits=[(sDVE, v2)] + list(waits3),
                       inc=sDVE)

    def quant_chain(bufs, F, src, soff, dst, doff, waits1, waits3):
        tok = quant_s1(bufs, F, src, soff, waits1)
        return quant_s2(tok, dst, doff, waits3)

    with root, nc.Block() as block:
        stTmp = ExitStack()
        idn_f = mksb(stTmp, "idn_f", [128, 128], F32, side=R)
        cos_f = mksb(stTmp, "cos_f", [128, NT, 32], F32, side=R)
        sin_f = mksb(stTmp, "sin_f", [128, NT, 32], F32, side=R)
        msk_f = mksb(stTmp, "msk_f", [128, 4, 512], F32, side=R)
        bst = mksb(stTmp, "bst", [1, NP * 384], F32, side=R)

        stCst = ExitStack()
        cosb = mksb(stCst, "cosb", [128, NT, 32], BF16)
        sinb = mksb(stCst, "sinb", [128, NT, 32], BF16)
        mskb = mksb(stCst, "mskb", [128, 4, 512], BF16)
        bqkvs = mksb(stCst, "bqkvs", [1, NP * 384], BF16)
        bos = mksb(stCst, "bos", [1, D], BF16)

        dma("sync", sDMA, cos_f[:], cos_h.rearrange("(n p) j -> p n j", p=128))
        dma("sync", sDMA, sin_f[:], sin_h.rearrange("(n p) j -> p n j", p=128))
        dma("sync", sDMA, idn_f[:], idn_h)
        w_pre = dma("sync", sDMA, msk_f[:],
                    msk_h.rearrange("p (m f) -> p m f", f=512))

        def cast_consts(v):
            v.memset(epsc[:], 1e-5)
            v.memset(nb12[:], -12.0)
            v.memset(ones1[:], 1.0)
            v.tensor_copy(idn[:], idn_f[:])
            v.tensor_copy(cosb[:], cos_f[:])
            v.tensor_copy(sinb[:], sin_f[:])
            return v.tensor_copy(mskb[:], msk_f[:])
        v_consts = sch.add("vector", cast_consts, waits=[(sDMA, w_pre)],
                           inc=sDVE)

        wb1 = dma("sync", sDMA, bst[:], bqkv_h)
        ab1 = sch.add("scalar", (lambda c: c.copy(bqkvs[:], bst[:])),
                      waits=[(sDMA, wb1)], inc=sACT)
        wb2 = dma("sync", sDMA, bst.ap()[:, 0:D], bo_h, waits=[(sACT, ab1)])
        sch.add("scalar", (lambda c: c.copy(bos[:], bst.ap()[:, 0:D])),
                waits=[(sDMA, wb2)], inc=sACT)
        bias_cast_done = sACT.n
        sch.run(block)
        stTmp.close()

        stXlT = ExitStack()
        xlT = mksb(stXlT, "xlT", [128, KC, S], BF16, side=R)
        stLN = ExitStack()
        x_a = [mksb(stLN, f"x_a{i}", [128, D], F32, side=R) for i in range(2)]
        sqt = mksb(stLN, "sqt", [128, D], F32, side=R)
        mv = [mksb(stLN, f"mv{i}", [128, 6], F32, side=R) for i in range(2)]
        rinv1 = [mksb(stLN, f"rinv1_{i}", [128, 1], F32, side=R)
                 for i in range(2)]
        rstd1 = [mksb(stLN, f"rstd1_{i}", [128, 1], F32, side=R)
                 for i in range(2)]
        negm1 = [mksb(stLN, f"negm1_{i}", [128, 1], F32, side=R)
                 for i in range(2)]
        xl1 = [mksb(stLN, f"xl1_{i}", [128, D], BF16, side=R) for i in range(2)]

        evacA = {}
        aapA = {}
        vsumA = {}
        for t in range(NT):
            r = t % 2
            wx = []
            if t >= 2:
                wx += [(sACT, aapA[t - 2]), (sDVE, vsumA[t - 2])]
            wxd = dma("sync", sDMA, x_a[r][:], x_v[:, t, :], waits=wx)
            vsumA[t] = sch.add("vector", (lambda v, r=r: v.reduce_sum(
                mv[r][:, 0:1], x_a[r][:], AX.X)),
                waits=[(sDMA, wxd)], inc=sDVE)
            asq = sch.add("scalar", (lambda c, r=r: c.activation(
                sqt[:], x_a[r][:], AFT.Square, accum_out=mv[r][:, 1:2])),
                waits=[(sDMA, wxd)], inc=sACT)

            def stat2(v, r=r):
                v.tensor_scalar(mv[r][:, 2:3], mv[r][:, 0:1], 1.0 / D, None,
                                Alu.mult)
                return v.tensor_scalar(mv[r][:, 3:4], mv[r][:, 1:2], 1.0 / D,
                                       1e-5, Alu.mult, Alu.add)
            v2a = sch.add("vector", stat2,
                          waits=[(sACT, asq), (sDVE, vsumA[t])], inc=sDVE)
            v2b = sch.add("vector", (lambda v, r=r: v.tensor_tensor(
                mv[r][:, 4:5], mv[r][:, 2:3], mv[r][:, 2:3], Alu.mult)),
                waits=[(sDVE, v2a)], inc=sDVE)
            vst = sch.add("vector", (lambda v, r=r: v.tensor_tensor(
                mv[r][:, 5:6], mv[r][:, 3:4], mv[r][:, 4:5], Alu.subtract)),
                waits=[(sDVE, v2b)], inc=sDVE)
            vrc = sch.add("vector", (lambda v, r=r: v.reciprocal(
                rinv1[r][:], mv[r][:, 5:6])), waits=[(sDVE, vst)], inc=sDVE)
            ast = sch.add("scalar", (lambda c, r=r: c.activation(
                rstd1[r][:], rinv1[r][:], AFT.Sqrt)),
                waits=[(sDVE, vrc)], inc=sACT)
            vng = sch.add("vector", (lambda v, r=r: v.tensor_scalar(
                negm1[r][:], mv[r][:, 2:3], rstd1[r].ap(), -1.0,
                Alu.mult, Alu.mult)),
                waits=[(sACT, ast)], inc=sDVE)
            aapA[t] = sch.add("scalar", (lambda c, r=r: c.activation(
                xl1[r][:], x_a[r][:], AFT.Identity,
                bias=negm1[r].ap(), scale=rstd1[r].ap())),
                waits=[(sDVE, vng)], inc=sACT)

            def trsA(te, r=r):
                ins = None
                for j in range(KC):
                    ins = te.transpose(psTs[r][:, j * 128:(j + 1) * 128],
                                       xl1[r][:, j * 128:(j + 1) * 128],
                                       idn[:])
                return ins
            wtr = [(sACT, aapA[t]), (sDVE, v_consts)]
            if t >= 2:
                wtr.append((sACT, evacA[t - 2]))
            ptr = sch.add("tensor", trsA, waits=wtr, inc=sPE)
            evacA[t] = sch.add("scalar", (lambda c, t=t, r=r: c.copy(
                xlT.ap()[:, :, t * 128:(t + 1) * 128],
                psTs[r].ap().rearrange("p (j f) -> p j f", f=128))),
                waits=[(sPE, ptr)], inc=sACT)
        phaseA_act = sACT.n
        sch.run(block)
        stLN.close()

        stEq = ExitStack()
        FQ = 1024
        wfE = [mksb(stEq, f"wfE{i}", [128, FQ], F32, side=R) for i in range(2)]
        quE = mksb(stEq, "quE", [128, FQ], F32, side=R)
        qtE = mksb(stEq, "qtE", [128, FQ], F32, side=R)
        qcA = mksb(stEq, "qcA", [128, FQ], F32, side=R)
        qcB = mksb(stEq, "qcB", [128, FQ], F32, side=R)
        qcC = mksb(stEq, "qcC", [128, FQ], F32, side=R)
        qsg = mksb(stEq, "qsg", [128, FQ], BF16, side=R)
        qs2 = mksb(stEq, "qs2", [128, FQ], BF16, side=R)
        qqE = [mksb(stEq, f"qqE{i}", [128, FQ], BF16, side=R)
               for i in range(2)]
        qbm = mksb(stEq, "qbm", [128, FQ // 16], F32, side=R)
        qsh = mksb(stEq, "qsh", [128, FQ // 16], F32, side=R)
        qrb = mksb(stEq, "qrb", [128, FQ // 16], F32, side=R)
        qbc = mksb(stEq, "qbc", [128, FQ // 16], F32, side=R)
        qr2 = mksb(stEq, "qr2", [128, FQ // 16], F32, side=R)
        qxs = mksb(stEq, "qxs", [128, FQ], F32, side=R)
        bufsE = (quE, qtE, qcA, qcB, qcC, qsg, qs2, qbm, qsh, qrb, qbc,
                 qr2, qxs)

        wjobs = []
        for rt in range(NJ):
            wjobs.append((wg_h, wgq_raw, rt, 0, 1024))
        for rt in range(NJ):
            wjobs.append((wu_h, wuq_raw, rt, 0, 1024))
        for rt in range(KC):
            for cc, F in ((0, 1024), (1024, 1024), (2048, 768)):
                wjobs.append((wd_h, wdq_raw, rt, cc, F))

        e1 = {"ji": 0, "stage2": None, "prev_chain": 0, "out_dma": {},
              "aq_prev": {}}
        a_done_dve = sDVE.n

        def e1_stage1():
            ji = e1["ji"]
            if ji >= len(wjobs):
                return
            wh, raw, rt, cc, F = wjobs[ji]
            r = ji % 2
            ws = ([(sDVE, e1["prev_chain"])] if ji >= 1
                  else [(sACT, phaseA_act), (sDVE, a_done_dve)])
            wdma = dma("sync", sDMA, wfE[r][:, 0:F],
                       wh[rt * 128:(rt + 1) * 128, cc:cc + F], waits=ws)
            tok = quant_s1(bufsE, F, wfE[r], 0, [(sDMA, wdma)])
            e1["stage2"] = (ji, raw, rt, cc, F, tok)
            e1["ji"] = ji + 1

        def e1_do_stage2():
            if e1["stage2"] is None:
                return
            ji, raw, rt, cc, F, tok = e1["stage2"]
            e1["stage2"] = None
            qr = ji % 2
            w3 = []
            if ji >= 2 and (ji - 2) in e1["out_dma"]:
                w3.append((sSCR, e1["out_dma"][ji - 2]))
            qv = quant_s2(tok, qqE[qr], 0, w3)
            e1["prev_chain"] = qv
            e1["out_dma"][ji] = dma(
                "gpsimd", sSCR, raw[rt * 128:(rt + 1) * 128, cc:cc + F],
                qqE[qr].ap()[:, 0:F], waits=[(sDVE, qv)])

        def e1_pump():
            e1_do_stage2()
            e1_stage1()

        stAttnT = ExitStack()
        attnT = mksb(stAttnT, "attnT", [128, NP, S], BF16)
        stB = ExitStack()
        wstg = mksb(stB, "wstg", [128, KC, 384], F32, side=R)
        wqkvp = mksb(stB, "wqkvp", [128, KC, 384], BF16, side=R)
        q_tok = mksb(stB, "q_tok", [128, NT, 128], BF16, side=R)
        k_tok = mksb(stB, "k_tok", [128, NT, 128], BF16, side=R)
        vaug = mksb(stB, "vaug", [128, NT, 2, 65], BF16, side=R)
        tAb = mksb(stB, "tAb", [128, NT, 2, 2, 32], BF16, side=R)
        tBb = mksb(stB, "tBb", [128, NT, 2, 2, 32], BF16, side=R)
        QTr = mksb(stB, "QTr", [128, S], BF16, side=R)
        KTr = mksb(stB, "KTr", [128, S], BF16, side=R)
        PT = mksb(stB, "PT", [128, 4, 512], BF16, side=R)
        att65 = [mksb(stB, f"att65_{i}", [65, 512], BF16, side=R)
                 for i in range(2)]
        attok = mksb(stB, "attok", [128, NT, 65], BF16, side=R)
        attokn = mksb(stB, "attokn", [128, NT, 64], BF16, side=R)
        rd16 = mksb(stB, "rd16", [128, NT], F32, side=R)

        v_ones = sch.add("vector", (lambda v: v.memset(
            vaug.ap()[:, :, :, 64:65], 1.0)),
            waits=[(sDVE, v_consts)], inc=sDVE)

        sc_banks = [psC, psD]
        av_banks = [psO.ap()[0:65, 0:512], psO.ap()[0:65, 512:1024]]
        sc_last_exp = [0, 0]
        av_last_evac = [0, 0]
        sc_ctr = 0
        wcast_done = 0
        attok_prev = 0
        t2_prev_pe = 0
        for pr in range(NP):
            ws = ([(sDVE, wcast_done)] if wcast_done
                  else [(sACT, phaseA_act)])
            wdma = dma("sync", sDMA, wstg[:],
                       wqkv_h[:, pr * 384:(pr + 1) * 384]
                       .rearrange("(c p) n -> p c n", p=128), waits=ws)
            wcast_done = sch.add("vector", (lambda v: v.tensor_copy(
                wqkvp.ap().rearrange("p c n -> p (c n)"),
                wstg.ap().rearrange("p c n -> p (c n)"))),
                waits=[(sDMA, wdma)], inc=sDVE)

            evacB = {}
            for t in range(NT):
                ps = [psA, psB][t % 2]

                def qkv(te, t=t, ps=ps, pr=pr):
                    for kc in range(KC):
                        te.matmul(ps[:, 0:384],
                                  xlT.ap()[:, kc, t * 128:(t + 1) * 128],
                                  wqkvp.ap()[:, kc, :],
                                  start=(kc == 0), stop=False)
                    return te.matmul(ps[:, 0:384], ones1.ap(),
                                     bqkvs.ap()[:, pr * 384:(pr + 1) * 384],
                                     start=False, stop=True)
                waits = [(sDVE, wcast_done), (sACT, bias_cast_done),
                         (sACT, phaseA_act)]
                if t >= 2:
                    waits.append((sACT, evacB[t - 2]))
                elif pr > 0:
                    waits.append((sACT, sACT.n))
                pm = sch.add("tensor", qkv, waits=waits, inc=sPE)

                def ev(c, t=t, ps=ps):
                    c.copy(q_tok[:, t, :], ps[:, 0:128])
                    c.copy(k_tok[:, t, :], ps[:, 128:256])
                    return c.copy(
                        vaug.ap()[:, t, :, 0:64],
                        ps.ap()[:, 256:384]
                        .rearrange("p (a l) -> p a l", l=64))
                evacB[t] = sch.add("scalar", ev, waits=[(sPE, pm)], inc=sACT)
            qkv_evac = sACT.n
            e1_pump()

            def rope(v, src):
                sv = src.ap().rearrange("p n (h x j) -> p n h x j",
                                        h=2, x=2, j=32)
                cb = cosb.ap().unsqueeze(2).unsqueeze(3).broadcast_to(
                    [128, NT, 2, 1, 32])
                sbb = sinb.ap().unsqueeze(2).unsqueeze(3).broadcast_to(
                    [128, NT, 2, 1, 32])
                x0 = sv[:, :, :, 0:1, :]
                x1 = sv[:, :, :, 1:2, :]
                v.tensor_tensor(tAb.ap()[:, :, :, 0:1, :], x0, cb, Alu.mult)
                v.tensor_tensor(tBb.ap()[:, :, :, 0:1, :], x1, sbb, Alu.mult)
                v.tensor_tensor(tAb.ap()[:, :, :, 1:2, :], x0, sbb, Alu.mult)
                v.tensor_tensor(tBb.ap()[:, :, :, 1:2, :], x1, cb, Alu.mult)
                v.tensor_tensor(sv[:, :, :, 0:1, :], tAb.ap()[:, :, :, 0:1, :],
                                tBb.ap()[:, :, :, 0:1, :], Alu.subtract)
                return v.tensor_tensor(sv[:, :, :, 1:2, :],
                                       tAb.ap()[:, :, :, 1:2, :],
                                       tBb.ap()[:, :, :, 1:2, :], Alu.add)
            vr_q = sch.add("vector", (lambda v: rope(v, q_tok)),
                           waits=[(sACT, qkv_evac)], inc=sDVE)
            vr_k = sch.add("vector", (lambda v: rope(v, k_tok)),
                           waits=[(sACT, qkv_evac)], inc=sDVE)

            tr_evac = {}
            for which, src, dstT, vw in (("q", q_tok, QTr, vr_q),
                                         ("k", k_tok, KTr, vr_k)):
                for g in range(2):
                    ps = psTs[g % 2]

                    def trs(te, src=src, g=g, ps=ps):
                        ins = None
                        for jj in range(8):
                            t = g * 8 + jj
                            ins = te.transpose(ps[:, jj * 128:(jj + 1) * 128],
                                               src[:, t, :], idn[:])
                        return ins
                    waits = [(sDVE, vw)]
                    if (which, g) == ("q", 0):
                        waits.append((sACT, qkv_evac))
                        if pr == 0:
                            waits.append((sACT, evacA[NT - 1]))
                        else:
                            waits.append((sACT, attok_prev))
                    if tr_evac:
                        waits.append((sACT, max(tr_evac.values())))
                    ptr = sch.add("tensor", trs, waits=waits, inc=sPE)
                    tr_evac[(which, g)] = sch.add("scalar", (
                        lambda c, dstT=dstT, g=g, ps=ps: c.copy(
                            dstT.ap()[:, g * 1024:(g + 1) * 1024], ps[:])),
                        waits=[(sPE, ptr)], inc=sACT)
            trq_done = sACT.n

            for hh in range(2):
                po = 64 * hh
                for qb in range(4):
                    nkc = 4 * qb + 4
                    abk = (pr * 8 + hh * 4 + qb) % 2
                    exp_of = {}
                    av_of = {}
                    pend = []
                    for kc in range(nkc):
                        bk = sc_ctr % 2
                        sc_ctr += 1

                        def smm(te, kc=kc, qb=qb, po=po, ps=sc_banks[bk]):
                            return te.matmul(
                                ps[:],
                                KTr.ap()[po:po + 64, kc * 128:(kc + 1) * 128],
                                QTr.ap()[po:po + 64, qb * 512:(qb + 1) * 512],
                                start=True, stop=True)
                        waits = [(sACT, trq_done)]
                        if sc_last_exp[bk]:
                            waits.append((sACT, sc_last_exp[bk]))
                        pm = sch.add("tensor", smm, waits=waits, inc=sPE)
                        pend.append((kc, bk, pm))
                        if len(pend) >= 2:
                            kcp, bkp, pmp = pend.pop(0)
                            ew = [(sPE, pmp)]
                            if kcp >= 4:
                                ew.append((sPE, av_of[kcp - 4]))
                            exp_of[kcp] = sch.add("scalar", (
                                lambda c, kcp=kcp, ps=sc_banks[bkp]:
                                c.activation(
                                    PT[:, kcp % 4, :], ps[:], AFT.Exp,
                                    bias=nb12.ap(), scale=0.125)),
                                waits=ew, inc=sACT)
                            sc_last_exp[bkp] = exp_of[kcp]
                            mneed = kcp >= 4 * qb
                            vmk = None
                            if mneed:
                                vmk = sch.add("vector", (
                                    lambda v, kcp=kcp, qb=qb: v.tensor_tensor(
                                        PT[:, kcp % 4, :], PT[:, kcp % 4, :],
                                        mskb.ap()[:, kcp - 4 * qb, :],
                                        Alu.mult)),
                                    waits=[(sACT, exp_of[kcp])], inc=sDVE)

                            def avm(te, kcp=kcp, hh=hh, nkc=nkc, abk=abk):
                                return te.matmul(
                                    av_banks[abk], vaug.ap()[:, kcp, hh, :],
                                    PT[:, kcp % 4, :],
                                    start=(kcp == 0), stop=(kcp == nkc - 1),
                                    skip_group_check=True)
                            avw = ([(sDVE, vmk)] if mneed
                                   else [(sACT, exp_of[kcp])])
                            if kcp == 0:
                                avw.append((sDVE, v_ones))
                                if av_last_evac[abk]:
                                    avw.append((sACT, av_last_evac[abk]))
                            av_of[kcp] = sch.add("tensor", avm, waits=avw,
                                                 inc=sPE)
                    while pend:
                        kcp, bkp, pmp = pend.pop(0)
                        ew = [(sPE, pmp)]
                        if kcp >= 4:
                            ew.append((sPE, av_of[kcp - 4]))
                        exp_of[kcp] = sch.add("scalar", (
                            lambda c, kcp=kcp, ps=sc_banks[bkp]:
                            c.activation(
                                PT[:, kcp % 4, :], ps[:], AFT.Exp,
                                bias=nb12.ap(), scale=0.125)),
                            waits=ew, inc=sACT)
                        sc_last_exp[bkp] = exp_of[kcp]
                        mneed = kcp >= 4 * qb
                        vmk = None
                        if mneed:
                            vmk = sch.add("vector", (
                                lambda v, kcp=kcp, qb=qb: v.tensor_tensor(
                                    PT[:, kcp % 4, :], PT[:, kcp % 4, :],
                                    mskb.ap()[:, kcp - 4 * qb, :], Alu.mult)),
                                waits=[(sACT, exp_of[kcp])], inc=sDVE)

                        def avm(te, kcp=kcp, hh=hh, nkc=nkc, abk=abk):
                            return te.matmul(
                                av_banks[abk], vaug.ap()[:, kcp, hh, :],
                                PT[:, kcp % 4, :],
                                start=(kcp == 0), stop=(kcp == nkc - 1),
                                skip_group_check=True)
                        avw = ([(sDVE, vmk)] if mneed
                               else [(sACT, exp_of[kcp])])
                        if kcp == 0:
                            avw.append((sDVE, v_ones))
                            if av_last_evac[abk]:
                                avw.append((sACT, av_last_evac[abk]))
                        av_of[kcp] = sch.add("tensor", avm, waits=avw,
                                             inc=sPE)
                    a65b = att65[abk]
                    av_last_evac[abk] = sch.add("scalar", (
                        lambda c, a65b=a65b, abk=abk: c.copy(
                            a65b[:], av_banks[abk])),
                        waits=[(sPE, av_of[nkc - 1])], inc=sACT)

                    def t1s(te, a65b=a65b):
                        ins = None
                        for qc in range(4):
                            ins = te.transpose(
                                psT1[:, qc * 128:qc * 128 + 65],
                                a65b.ap()[:, qc * 128:(qc + 1) * 128],
                                idn.ap()[0:65, 0:65])
                        return ins
                    wt1 = [(sACT, av_last_evac[abk])]
                    if attok_prev:
                        wt1.append((sACT, attok_prev))
                    pt1 = sch.add("tensor", t1s, waits=wt1, inc=sPE)
                    attok_prev = sch.add("scalar", (lambda c, qb=qb: c.copy(
                        attok.ap()[:, qb * 4:(qb + 1) * 4, :],
                        psT1.ap()[:, 0:512]
                        .rearrange("p (qc f) -> p qc f", f=128)
                        [:, :, 0:65])),
                        waits=[(sPE, pt1)], inc=sACT)
                    if qb >= 1:
                        e1_pump()
                all_attok = sACT.n
                vrd = sch.add("vector", (lambda v: v.reciprocal(
                    rd16[:], attok.ap()[:, :, 64])),
                    waits=[(sACT, all_attok), (sPE, t2_prev_pe)], inc=sDVE)
                vnorm = sch.add("vector", (lambda v: v.tensor_tensor(
                    attokn[:], attok.ap()[:, :, 0:64],
                    rd16.ap().unsqueeze(2).broadcast_to([128, NT, 64]),
                    Alu.mult)), waits=[(sDVE, vrd)], inc=sDVE)
                for g in range(2):
                    def t2s(te, g=g):
                        ins = None
                        for jj in range(8):
                            qt = g * 8 + jj
                            ins = te.transpose(
                                psT2[0:64, jj * 128:(jj + 1) * 128],
                                attokn[:, qt, :], idn[:])
                        return ins
                    wt2 = [(sDVE, vnorm), (sACT, sACT.n)]
                    t2_prev_pe = sch.add("tensor", t2s, waits=wt2, inc=sPE)
                    sch.add("scalar", (lambda c, g=g, po=po, pr=pr: c.copy(
                        attnT.ap()[po:po + 64, pr, g * 1024:(g + 1) * 1024]
                        .rearrange("p (jj f) -> p jj f", f=128),
                        psT2.ap()[0:64, :]
                        .rearrange("p (jj f) -> p jj f", f=128))),
                        waits=[(sPE, t2_prev_pe)], inc=sACT)
                e1_pump()
        stB_attn_done = sACT.n
        stB_dve_done = sDVE.n
        dbg("d_attnT", [128, NP, S], BF16, attnT[:], [(sACT, stB_attn_done)])
        sch.run(block)
        stB.close()

        stC = ExitStack()
        wostg = mksb(stC, "wostg", [128, KC, 512], F32, side=R)
        wos = mksb(stC, "wos", [128, KC, D], BF16, side=R)
        x_c = [mksb(stC, f"x_c{i}", [128, D], F32, side=R) for i in range(2)]
        x1b = [mksb(stC, f"x1b{i}", [128, D], F32, side=R) for i in range(2)]
        sqt2 = mksb(stC, "sqt2", [128, D], F32, side=R)
        mv2 = [mksb(stC, f"mv2_{i}", [128, 6], F32, side=R) for i in range(2)]
        rt2 = [mksb(stC, f"rt2_{i}", [128, 2], F32, side=R) for i in range(2)]

        for nh in range(2):
            wdma = dma("sync", sDMA, wostg[:],
                       wo_h[:, nh * 512:(nh + 1) * 512]
                       .rearrange("(c p) n -> p c n", p=128),
                       waits=([(sDVE, sDVE.n)] if nh
                              else [(sACT, stB_attn_done),
                                    (sDVE, stB_dve_done)]))
            sch.add("vector", (lambda v, nh=nh: v.tensor_copy(
                wos.ap()[:, :, nh * 512:(nh + 1) * 512], wostg.ap())),
                waits=[(sDMA, wdma)], inc=sDVE)
        wo_cast = sDVE.n

        epiC = {}
        dumpC = {}
        aC = {}
        vC = {}
        for t in range(NT):
            r = t % 2
            wx = [(sACT, stB_attn_done)]
            if t >= 2:
                wx += [(sDVE, vC[t - 2]), (sACT, aC[t - 2])]
            wxd = dma("sync", sDMA, x_c[r][:], x_v[:, t, :], waits=wx)

            def omm(te, t=t):
                ins = None
                for nh in range(2):
                    for pr in range(NP):
                        te.matmul(psO[:, nh * 512:(nh + 1) * 512],
                                  attnT.ap()[:, pr, t * 128:(t + 1) * 128],
                                  wos.ap()[:, pr, nh * 512:(nh + 1) * 512],
                                  start=(pr == 0), stop=False)
                    ins = te.matmul(psO[:, nh * 512:(nh + 1) * 512],
                                    ones1.ap(),
                                    bos.ap()[:, nh * 512:(nh + 1) * 512],
                                    start=False, stop=True)
                return ins
            waits = [(sACT, stB_attn_done), (sDVE, wo_cast)]
            if t >= 1:
                waits.append((sDVE, epiC[t - 1]))
            pm = sch.add("tensor", omm, waits=waits, inc=sPE)
            epw = [(sPE, pm), (sDMA, wxd)]
            if t >= 2:
                epw.append((sSCR, dumpC[t - 2]))
            epiC[t] = sch.add("vector", (lambda v, t=t, r=r: v.tensor_tensor(
                x1b[r][:], x_c[r][:], psO[:], Alu.add)), waits=epw, inc=sDVE)
            dumpC[t] = dma("gpsimd", sSCR, xscr_v[:, t, :], x1b[r][:],
                           waits=[(sDVE, epiC[t])])
            vsum = sch.add("vector", (lambda v, r=r: v.reduce_sum(
                mv2[r][:, 0:1], x1b[r][:], AX.X)),
                waits=[(sDVE, epiC[t])], inc=sDVE)
            asq = sch.add("scalar", (lambda c, r=r: c.activation(
                sqt2[:], x1b[r][:], AFT.Square, accum_out=mv2[r][:, 1:2])),
                waits=[(sDVE, epiC[t])], inc=sACT)

            def stat2d(v, t=t, r=r):
                v.tensor_scalar(mv2[r][:, 2:3], mv2[r][:, 0:1], 1.0 / D, None,
                                Alu.mult)
                v.tensor_scalar(negm_all[:, t:t + 1], mv2[r][:, 0:1],
                                -1.0 / D, None, Alu.mult)
                return v.tensor_scalar(mv2[r][:, 3:4], mv2[r][:, 1:2],
                                       1.0 / D, 1e-5, Alu.mult, Alu.add)
            v2a = sch.add("vector", stat2d, waits=[(sACT, asq), (sDVE, vsum)],
                          inc=sDVE)
            v2b = sch.add("vector", (lambda v, r=r: v.tensor_tensor(
                mv2[r][:, 4:5], mv2[r][:, 2:3], mv2[r][:, 2:3], Alu.mult)),
                waits=[(sDVE, v2a)], inc=sDVE)
            vst = sch.add("vector", (lambda v, r=r: v.tensor_tensor(
                rt2[r][:, 0:1], mv2[r][:, 3:4], mv2[r][:, 4:5],
                Alu.subtract)), waits=[(sDVE, v2b)], inc=sDVE)
            vC[t] = sch.add("vector", (lambda v, r=r: v.reciprocal(
                rt2[r][:, 1:2], rt2[r][:, 0:1])), waits=[(sDVE, vst)],
                inc=sDVE)
            aC[t] = sch.add("scalar", (lambda c, t=t, r=r: c.activation(
                rstd_all[:, t:t + 1], rt2[r][:, 1:2], AFT.Sqrt)),
                waits=[(sDVE, vC[t])], inc=sACT)
            e1_pump()
            e1_pump()
        while e1["ji"] < len(wjobs) or e1["stage2"] is not None:
            e1_pump()
        e1_done_scr = sSCR.n
        e1_done_dve = sDVE.n
        ln2_done_act = sACT.n
        ln2_done_vec = sDVE.n
        xdump_done = sSCR.n
        dbg("d_rstd", [128, NT], F32, rstd_all[:], [(sACT, ln2_done_act)])
        dbg("d_negm", [128, NT], F32, negm_all[:], [(sDVE, ln2_done_vec)])
        sch.run(block)
        stC.close()
        stEq.close()
        stXlT.close()
        stAttnT.close()
        stCst.close()

        stE = ExitStack()
        wgqT = mksb(stE, "wgqT", [128, KC, DFF], BF16)
        wuqT = mksb(stE, "wuqT", [128, KC, DFF], BF16)
        wdqT = mksb(stE, "wdqT", [128, NJ, D], BF16)
        stEc = ExitStack()
        ebuf = [mksb(stEc, f"ebuf{i}", [128, DFF], BF16, side=R)
                for i in range(2)]

        tjobs = []
        for rt in range(NJ):
            tjobs.append((wgq_raw, wgqT, "gu", rt, 1024))
        for rt in range(NJ):
            tjobs.append((wuq_raw, wuqT, "gu", rt, 1024))
        for rt in range(KC):
            tjobs.append((wdq_raw, wdqT, "d", rt, DFF))

        prev_evacE = 0
        prev_ldE = {}
        for ji, (raw, dstT, kind, rt, F) in enumerate(tjobs):
            r = ji % 2
            ws = [(sDVE, ln2_done_vec), (sACT, ln2_done_act),
                  (sSCR, e1_done_scr)] if ji < 2 else [(sACT, prev_ldE[ji - 2])]
            wdma = dma("sync", sDMA, ebuf[r][:, 0:F],
                       raw[rt * 128:(rt + 1) * 128, 0:F], waits=ws)
            njj = F // 128
            for half in range((njj + 7) // 8):
                j0 = half * 8
                jn = min(8, njj - j0)
                pst = psTs[half % 2]

                def trs(te, r=r, j0=j0, jn=jn, pst=pst):
                    ins = None
                    for jj in range(jn):
                        ins = te.transpose(
                            pst[:, jj * 128:(jj + 1) * 128],
                            ebuf[r][:, (j0 + jj) * 128:(j0 + jj + 1) * 128],
                            idn[:])
                    return ins
                wtr = [(sDMA, wdma)]
                if prev_evacE:
                    wtr.append((sACT, prev_evacE))
                ptr = sch.add("tensor", trs, waits=wtr, inc=sPE)
                if kind == "d":
                    def ev(c, rt=rt, j0=j0, jn=jn, pst=pst):
                        return c.copy(
                            wdqT.ap()[:, j0:j0 + jn,
                                      rt * 128:(rt + 1) * 128],
                            pst.ap()[:, 0:jn * 128]
                            .rearrange("p (jj f) -> p jj f", f=128))
                else:
                    def ev(c, dstT=dstT, rt=rt, j0=j0, jn=jn, pst=pst):
                        return c.copy(
                            dstT.ap()[:, j0:j0 + jn,
                                      rt * 128:(rt + 1) * 128],
                            pst.ap()[:, 0:jn * 128]
                            .rearrange("p (jj f) -> p jj f", f=128))
                prev_evacE = sch.add("scalar", ev, waits=[(sPE, ptr)],
                                     inc=sACT)
            prev_ldE[ji] = prev_evacE
        wq_done_act = sACT.n
        wq_done_vec = sDVE.n
        wq_done_pe = sPE.n
        dbg("d_wgqT", [128, KC, DFF], BF16, wgqT[:], [(sACT, wq_done_act)])
        dbg("d_wdqT", [128, NJ, D], BF16, wdqT[:], [(sACT, wq_done_act)])
        sch.run(block)
        stEc.close()

        stF = ExitStack()
        x_t = [mksb(stF, f"x_t{i}", [128, D], F32, side=R) for i in range(2)]
        xl2c = mksb(stF, "xl2c", [128, D], F32, side=R)
        xqb = mksb(stF, "xqb", [128, D], BF16, side=R)
        xqT_t = mksb(stF, "xqT_t", [128, KC, 128], BF16, side=R)
        silu_sb = [mksb(stF, f"silu{i}", [128, 512], F32, side=R)
                   for i in range(2)]
        up_sb = [mksb(stF, f"up{i}", [128, 512], F32, side=R)
                 for i in range(2)]
        hmat = [mksb(stF, f"hmat{i}", [128, 512], F32, side=R)
                for i in range(2)]
        hqb = [mksb(stF, f"hqb{i}", [128, 512], BF16, side=R)
               for i in range(2)]
        hqT_t = mksb(stF, "hqT_t", [128, NJ, 128], BF16, side=R)
        outb = [mksb(stF, f"outb{i}", [128, D], F32, side=R) for i in range(2)]
        FQ2 = 512
        bufsF = []
        for i in range(2):
            bufsF.append((
                mksb(stF, f"quF{i}", [128, FQ2], F32, side=R),
                mksb(stF, f"qtF{i}", [128, FQ2], F32, side=R),
                mksb(stF, f"qcAF{i}", [128, FQ2], F32, side=R),
                mksb(stF, f"qcBF{i}", [128, FQ2], F32, side=R),
                mksb(stF, f"qcCF{i}", [128, FQ2], F32, side=R),
                mksb(stF, f"qsgF{i}", [128, FQ2], BF16, side=R),
                mksb(stF, f"qs2F{i}", [128, FQ2], BF16, side=R),
                mksb(stF, f"qbmF{i}", [128, FQ2 // 16], F32, side=R),
                mksb(stF, f"qshF{i}", [128, FQ2 // 16], F32, side=R),
                mksb(stF, f"qrbF{i}", [128, FQ2 // 16], F32, side=R),
                mksb(stF, f"qbcF{i}", [128, FQ2 // 16], F32, side=R),
                mksb(stF, f"qr2F{i}", [128, FQ2 // 16], F32, side=R),
                mksb(stF, f"qxsF{i}", [128, FQ2], F32, side=R),
            ))

        ncl = [(0, 512), (512, 512), (1024, 512), (1536, 512), (2048, 512),
               (2560, 256)]
        g_banks = [psA, psC]
        u_banks = [psB, psD]
        g_last = [0, 0]
        u_last = [0, 0]
        chain_last_tr = {}
        prev_down = 0
        prev_epi = 0
        out_dma = {}
        gi = 0
        for t in range(NT):
            r = t % 2
            wx = [(sSCR, xdump_done)]
            if t < 2:
                wx += [(sDVE, wq_done_vec), (sACT, wq_done_act),
                       (sPE, wq_done_pe)]
            if t >= 2:
                wx.append((sDVE, prev_epi))
            wxd = dma("sync", sDMA, x_t[r][:], xscr_v[:, t, :], waits=wx)
            ac = sch.add("scalar", (lambda c, t=t, r=r: c.activation(
                xl2c[:], x_t[r][:], AFT.Identity,
                bias=negm_all.ap()[:, t:t + 1], scale=1.0)),
                waits=[(sDMA, wxd), (sACT, wq_done_act), (sACT, ln2_done_act)],
                inc=sACT)
            w30 = [(sPE, chain_last_tr[0])] if 0 in chain_last_tr else []
            q1 = quant_chain(bufsF[0], 512, xl2c, 0, xqb, 0,
                             [(sACT, ac)], w30)
            w31 = [(sPE, chain_last_tr[1])] if 1 in chain_last_tr else []
            q2 = quant_chain(bufsF[1], 512, xl2c, 512, xqb, 512, [], w31)

            def trx(te):
                ins = None
                for jj in range(KC):
                    ins = te.transpose(psT1[:, jj * 128:(jj + 1) * 128],
                                       xqb[:, jj * 128:(jj + 1) * 128], idn[:])
                return ins
            wtx = [(sDVE, q2), (sACT, sACT.n)]
            ptx = sch.add("tensor", trx, waits=wtx, inc=sPE)
            chain_last_tr[0] = ptx
            chain_last_tr[1] = ptx
            exq = sch.add("scalar", (lambda c: c.copy(
                xqT_t.ap().rearrange("p c f -> p (c f)"), psT1[:])),
                waits=[(sPE, ptx)], inc=sACT)

            pendF = []

            def plan_tail(ncc, nsz, cset):
                njj = nsz // 128
                pst = psTs[cset]
                tok = pend_qh[(ncc, cset)]
                w3h = ([(sPE, chain_last_tr[cset])]
                       if cset in chain_last_tr else [])
                qh = quant_s2(tok, hqb[cset], 0, w3h)

                def trh(te, njj=njj, cset=cset, pst=pst):
                    ins = None
                    for jj in range(njj):
                        ins = te.transpose(
                            pst[:, jj * 128:(jj + 1) * 128],
                            hqb[cset][:, jj * 128:(jj + 1) * 128], idn[:])
                    return ins
                wth = [(sDVE, qh), (sACT, sACT.n)]
                pth = sch.add("tensor", trh, waits=wth, inc=sPE)
                chain_last_tr[cset] = pth
                ehq = sch.add("scalar", (lambda c, ncc=ncc, njj=njj, pst=pst:
                              c.copy(
                    hqT_t.ap()[:, ncc // 128:ncc // 128 + njj, :],
                    pst.ap()[:, 0:njj * 128]
                    .rearrange("p (jj f) -> p jj f", f=128))),
                    waits=[(sPE, pth)], inc=sACT)

                def dmm(te, ncc=ncc, njj=njj):
                    ins = None
                    for jj in range(njj):
                        j = ncc // 128 + jj
                        for nh in range(2):
                            ins = te.matmul(
                                psO[:, nh * 512:(nh + 1) * 512],
                                hqT_t.ap()[:, j, :],
                                wdqT.ap()[:, j, nh * 512:(nh + 1) * 512],
                                start=(j == 0), stop=(j == NJ - 1),
                                skip_group_check=True)
                    return ins
                wd_ = [(sACT, ehq)]
                if ncc == 0 and t >= 1:
                    wd_.append((sDVE, prev_epi))
                return sch.add("tensor", dmm, waits=wd_, inc=sPE)

            pend_qh = {}
            for nci, (ncc, nsz) in enumerate(ncl):
                cset = gi % 2
                gb = g_banks[cset]
                ub = u_banks[cset]
                gi += 1

                def gmm(te, ncc=ncc, nsz=nsz, gb=gb):
                    ins = None
                    for kc in range(KC):
                        ins = te.matmul(gb[:, 0:nsz], xqT_t.ap()[:, kc, :],
                                        wgqT.ap()[:, kc, ncc:ncc + nsz],
                                        start=(kc == 0), stop=(kc == KC - 1))
                    return ins

                def umm(te, ncc=ncc, nsz=nsz, ub=ub):
                    ins = None
                    for kc in range(KC):
                        ins = te.matmul(ub[:, 0:nsz], xqT_t.ap()[:, kc, :],
                                        wuqT.ap()[:, kc, ncc:ncc + nsz],
                                        start=(kc == 0), stop=(kc == KC - 1))
                    return ins
                wg_ = [(sACT, exq)]
                if g_last[cset]:
                    wg_.append((sACT, g_last[cset]))
                pg = sch.add("tensor", gmm, waits=wg_, inc=sPE)
                wu_ = [(sACT, u_last[cset])] if u_last[cset] else []
                pu = sch.add("tensor", umm, waits=wu_, inc=sPE)
                if pendF:
                    prev_down = plan_tail(*pendF.pop(0))
                asl = sch.add("scalar", (lambda c, t=t, nsz=nsz, gb=gb,
                                         cset=cset: c.activation(
                    silu_sb[cset][:, 0:nsz], gb[:, 0:nsz], AFT.Silu,
                    scale=rstd_all.ap()[:, t:t + 1])),
                    waits=[(sPE, pg)], inc=sACT)
                g_last[cset] = asl
                aup = sch.add("scalar", (lambda c, t=t, nsz=nsz, ub=ub,
                                         cset=cset: c.activation(
                    up_sb[cset][:, 0:nsz], ub[:, 0:nsz], AFT.Identity,
                    scale=rstd_all.ap()[:, t:t + 1])),
                    waits=[(sPE, pu)], inc=sACT)
                u_last[cset] = aup
                vh = sch.add("vector", (lambda v, nsz=nsz, cset=cset:
                             v.tensor_tensor(
                    hmat[cset][:, 0:nsz], silu_sb[cset][:, 0:nsz],
                    up_sb[cset][:, 0:nsz], Alu.mult)),
                    waits=[(sACT, asl), (sACT, aup)], inc=sDVE)
                tok = quant_s1(bufsF[cset], nsz, hmat[cset], 0,
                               [(sDVE, vh)])
                pend_qh[(ncc, cset)] = tok
                pendF.append((ncc, nsz, cset))
            while pendF:
                prev_down = plan_tail(*pendF.pop(0))
            epw = [(sPE, prev_down)]
            if t >= 2:
                epw.append((sSCR, out_dma[t - 2]))
            prev_epi = sch.add("vector", (lambda v, r=r: v.tensor_tensor(
                outb[r][:], x_t[r][:], psO[:], Alu.add)), waits=epw, inc=sDVE)
            out_dma[t] = dma("gpsimd", sSCR, out_v[:, t, :], outb[r][:],
                             waits=[(sDVE, prev_epi)])
        final_out = sSCR.n

        sch.add("sync", (lambda c: c.wait_ge(sSCR.h, final_out)), waits=[])
        sch.run(block)
        stF.close()
        stE.close()
        nc._dbg_outs = dbg_outs
    return nc


_NC_CACHE = {}


def _get_nc():
    if "nc" not in _NC_CACHE:
        _NC_CACHE["nc"] = build_nc()
    return _NC_CACHE["nc"]


def _rope_tables():
    inv = 1.0 / 10000.0 ** (2.0 * np.arange(32, dtype=np.float64) / 64)
    pos = np.arange(S, dtype=np.float64)
    ang = pos[:, None] * inv[None, :]
    return (np.cos(ang).astype(np.float32), np.sin(ang).astype(np.float32))


def _perm():
    p = np.zeros(D, dtype=np.int64)
    for h in range(H):
        for j in range(32):
            p[h * 64 + j] = h * 64 + 2 * j
            p[h * 64 + 32 + j] = h * 64 + 2 * j + 1
    return p


def _masks():
    out = np.zeros((128, 4, 512), dtype=np.float32)
    kt = np.arange(128)[:, None]
    qt = np.arange(512)[None, :]
    for m in range(4):
        out[:, m, :] = (qt >= m * 128 + kt).astype(np.float32)
    return out.reshape(128, 4 * 512)


def kernel(L0_attn_x, L0_attn_Wq, L0_attn_bq, L0_attn_Wk, L0_attn_bk,
           L0_attn_Wv, L0_attn_bv, L0_attn_Wo, L0_attn_bo,
           L0_ffn_w_gate, L0_ffn_w_up, L0_ffn_w_down):
    x = np.ascontiguousarray(np.asarray(L0_attn_x, dtype=np.float32))
    perm = _perm()
    wq = np.asarray(L0_attn_Wq, dtype=np.float32)[:, perm]
    wk = np.asarray(L0_attn_Wk, dtype=np.float32)[:, perm]
    bq = np.asarray(L0_attn_bq, dtype=np.float32)[perm]
    bk = np.asarray(L0_attn_bk, dtype=np.float32)[perm]
    wv = np.asarray(L0_attn_Wv, dtype=np.float32)
    bv = np.asarray(L0_attn_bv, dtype=np.float32)
    wqkv = np.empty((D, NP * 384), dtype=np.float32)
    bqkv = np.empty((1, NP * 384), dtype=np.float32)
    for pr in range(NP):
        cs = slice(pr * 128, (pr + 1) * 128)
        wqkv[:, pr * 384 + 0:pr * 384 + 128] = wq[:, cs]
        wqkv[:, pr * 384 + 128:pr * 384 + 256] = wk[:, cs]
        wqkv[:, pr * 384 + 256:pr * 384 + 384] = wv[:, cs]
        bqkv[0, pr * 384 + 0:pr * 384 + 128] = bq[cs]
        bqkv[0, pr * 384 + 128:pr * 384 + 256] = bk[cs]
        bqkv[0, pr * 384 + 256:pr * 384 + 384] = bv[cs]
    wo = np.ascontiguousarray(np.asarray(L0_attn_Wo, dtype=np.float32))
    bo = np.asarray(L0_attn_bo, dtype=np.float32).reshape(1, D).copy()
    wg = np.ascontiguousarray(np.asarray(L0_ffn_w_gate, dtype=np.float32))
    wu = np.ascontiguousarray(np.asarray(L0_ffn_w_up, dtype=np.float32))
    wd = np.ascontiguousarray(np.asarray(L0_ffn_w_down, dtype=np.float32))
    cosv, sinv = _rope_tables()
    idn = np.eye(128, dtype=np.float32)
    msk = _masks()

    nc = _get_nc()
    in_maps = []
    for b in range(8):
        in_maps.append({
            "x": np.ascontiguousarray(x[b]),
            "wqkv": wqkv, "wo": wo, "bqkv": bqkv, "bo": bo,
            "wg": wg, "wu": wu, "wd": wd,
            "cosv": cosv, "sinv": sinv, "idn": idn, "msk": msk,
        })
    res = run_bass_kernel_spmd(nc, in_maps, core_ids=list(range(8)))
    return np.stack([np.asarray(r["out"], dtype=np.float32)
                     for r in res.results], axis=0)


# version 12
# speedup vs baseline: 1.2104x; 1.1255x over previous
import numpy as np
import ml_dtypes
import concourse.bass as bass
import concourse.mybir as mybir
from concourse.bass_utils import run_bass_kernel_spmd
from contextlib import ExitStack

F32 = mybir.dt.float32
BF16 = mybir.dt.bfloat16
Alu = mybir.AluOpType
AFT = mybir.ActivationFunctionType
AX = mybir.AxisListType

S, D, H = 2048, 1024, 16
NT = S // 128
KC = D // 128
NP = H // 2
DFF = 2816
NJ = DFF // 128

_MIDS = [0.25, 0.75, 1.25, 1.75, 2.5, 3.5, 5.0]
TH = [float(np.float32((m / 6.0) ** 2)) for m in _MIDS]


class Sem:
    def __init__(self, handle):
        self.h = handle
        self.n = 0


class Sched:
    ENGS = ("sync", "tensor", "scalar", "vector", "gpsimd")

    def __init__(self):
        self.steps = {e: [] for e in self.ENGS}

    def add(self, eng, emit, waits=(), inc=None, inc_amt=1):
        self.steps[eng].append((list(waits), emit, inc, inc_amt))
        if inc is not None:
            inc.n += inc_amt
            return inc.n
        return None

    def run(self, block):
        for eng in self.ENGS:
            steps = self.steps[eng]
            if not steps:
                continue

            def mk(steps):
                def f(ctx):
                    for waits, emit, inc, amt in steps:
                        for sem, val in waits:
                            ctx.wait_ge(sem.h, val)
                        ins = emit(ctx)
                        if inc is not None:
                            ins.then_inc(inc.h, amt)
                return f
            getattr(block, eng)(mk(steps))
        self.steps = {e: [] for e in self.ENGS}


def build_nc(debug=False):
    nc = bass.Bass("TRN2", target_bir_lowering=False)

    x_h = nc.dram_tensor("x", [S, D], F32, kind="ExternalInput").ap()
    wqkv_h = nc.dram_tensor("wqkv", [D, NP * 384], F32,
                            kind="ExternalInput").ap()
    wo_h = nc.dram_tensor("wo", [D, D], F32, kind="ExternalInput").ap()
    bqkv_h = nc.dram_tensor("bqkv", [1, NP * 384], F32,
                            kind="ExternalInput").ap()
    bo_h = nc.dram_tensor("bo", [1, D], F32, kind="ExternalInput").ap()
    wg_h = nc.dram_tensor("wg", [DFF, D], F32, kind="ExternalInput").ap()
    wu_h = nc.dram_tensor("wu", [DFF, D], F32, kind="ExternalInput").ap()
    wd_h = nc.dram_tensor("wd", [D, DFF], F32, kind="ExternalInput").ap()
    cos_h = nc.dram_tensor("cosv", [S, 32], F32, kind="ExternalInput").ap()
    sin_h = nc.dram_tensor("sinv", [S, 32], F32, kind="ExternalInput").ap()
    idn_h = nc.dram_tensor("idn", [128, 128], F32, kind="ExternalInput").ap()
    msk_h = nc.dram_tensor("msk", [128, 4 * 512], F32, kind="ExternalInput").ap()
    out_h = nc.dram_tensor("out", [S, D], F32, kind="ExternalOutput").ap()
    xscr = nc.dram_tensor("xscr", [S, D], F32, kind="Internal").ap()
    wgq_raw = nc.dram_tensor("wgq_raw", [DFF, D], BF16, kind="Internal").ap()
    wuq_raw = nc.dram_tensor("wuq_raw", [DFF, D], BF16, kind="Internal").ap()
    wdq_raw = nc.dram_tensor("wdq_raw", [D, DFF], BF16, kind="Internal").ap()
    x_v = x_h.rearrange("(n p) d -> p n d", p=128)
    xscr_v = xscr.rearrange("(n p) d -> p n d", p=128)
    out_v = out_h.rearrange("(n p) d -> p n d", p=128)

    root = ExitStack()

    def mksb(st, name, shape, dtype=F32, side=None):
        kw = {"side": side} if side else {}
        return st.enter_context(nc.sbuf_tensor(name, shape, dtype, **kw))

    R = "right"

    sDMA = Sem(root.enter_context(nc.semaphore("sDMA")))
    sSCR = Sem(root.enter_context(nc.semaphore("sSCR")))
    sCST = Sem(root.enter_context(nc.semaphore("sCST")))
    sBIA = Sem(root.enter_context(nc.semaphore("sBIA")))
    sWB = Sem(root.enter_context(nc.semaphore("sWB")))
    sWO = Sem(root.enter_context(nc.semaphore("sWO")))
    sXA = [Sem(root.enter_context(nc.semaphore(f"sXA{i}"))) for i in range(2)]
    sXC = [Sem(root.enter_context(nc.semaphore(f"sXC{i}"))) for i in range(2)]
    sXF = [Sem(root.enter_context(nc.semaphore(f"sXF{i}"))) for i in range(2)]
    sWE = [Sem(root.enter_context(nc.semaphore(f"sWE{i}"))) for i in range(2)]
    sEO = [Sem(root.enter_context(nc.semaphore(f"sEO{i}"))) for i in range(2)]
    sDP = [Sem(root.enter_context(nc.semaphore(f"sDP{i}"))) for i in range(2)]
    sEB = [Sem(root.enter_context(nc.semaphore(f"sEB{i}"))) for i in range(2)]
    sFO = [Sem(root.enter_context(nc.semaphore(f"sFO{i}"))) for i in range(2)]
    sPE = Sem(root.enter_context(nc.semaphore("sPE")))
    sACT = Sem(root.enter_context(nc.semaphore("sACT")))
    sDVE = Sem(root.enter_context(nc.semaphore("sDVE")))

    psA = root.enter_context(nc.psum_tensor("psA", [128, 512], F32))
    psB = root.enter_context(nc.psum_tensor("psB", [128, 512], F32))
    psC = root.enter_context(nc.psum_tensor("psC", [128, 512], F32))
    psD = root.enter_context(nc.psum_tensor("psD", [128, 512], F32))
    psO = root.enter_context(nc.psum_tensor("psO", [128, 1024], F32))
    psT1 = root.enter_context(nc.psum_tensor("psT1", [128, 1024], BF16))
    psT2 = root.enter_context(nc.psum_tensor("psT2", [128, 1024], BF16))
    psTs = [psT1, psT2]

    idn = mksb(root, "idn_b", [128, 128], BF16)
    ones1 = mksb(root, "ones1", [1, 128], BF16)
    epsc = mksb(root, "epsc", [128, 1], F32)
    nb12 = mksb(root, "nb12", [128, 1], F32)
    rstd_all = mksb(root, "rstd_all", [128, NT], F32)
    negm_all = mksb(root, "negm_all", [128, NT], F32)

    sch = Sched()

    def dma(eng, sem, out, in_, waits=()):
        def emit(ctx, out=out, in_=in_):
            return ctx.dma_start(out=out, in_=in_)
        return sch.add(eng, emit, waits=waits, inc=sem, inc_amt=16)

    dbg_outs = []

    def dbg(name, shape, dtype, src_ap, waits):
        if not debug:
            return
        hap = nc.dram_tensor(name, shape, dtype, kind="ExternalOutput").ap()
        dbg_outs.append(name)
        dma("gpsimd", sSCR, hap, src_ap, waits=list(waits))

    def quant_chain(bufs, F, src, soff, dst, doff, waits1, waits3):
        (qu, qt_, qa, qb_, qc_, sg, s2, bm, sh, rb, bmc, rb2, xsq) = bufs
        nb = F // 16

        def c1r(v):
            sv = src.ap()[:, soff:soff + F].rearrange(
                "p (b k) -> p b k", k=16)
            return v.tensor_reduce(bm[:, 0:nb], sv, AX.X, Alu.max,
                                   apply_absolute_value=True)
        v1r = sch.add("vector", c1r, waits=list(waits1), inc=sDVE)
        aq = sch.add("scalar", (lambda c: c.activation(
            xsq[:, 0:F], src.ap()[:, soff:soff + F], AFT.Square)),
            waits=list(waits1), inc=sACT)

        def c1(v):
            v.tensor_scalar(bmc[:, 0:nb], bm[:, 0:nb], 6e-12, None,
                            Alu.max)
            v.tensor_scalar(sh[:, 0:nb], bm[:, 0:nb], 6e-12, 1.0 / 12.0,
                            Alu.max, Alu.mult)
            return v.tensor_scalar(sg[:, 0:F], src.ap()[:, soff:soff + F],
                                   0.0, None, Alu.is_ge)
        v1a = sch.add("vector", c1, waits=[(sDVE, v1r)], inc=sDVE)
        v1 = sch.add("vector", (lambda v: v.reciprocal(
            rb[:, 0:nb], bmc[:, 0:nb])), waits=[(sDVE, v1a)], inc=sDVE)
        v1b = sch.add("vector", (lambda v: v.tensor_tensor(
            rb2[:, 0:nb], rb[:, 0:nb], rb[:, 0:nb], Alu.mult)),
            waits=[(sDVE, v1)], inc=sDVE)

        def c2(v):
            rbb = rb2.ap()[:, 0:nb].unsqueeze(2).broadcast_to(
                [128, nb, 16])
            return v.tensor_tensor(
                qu.ap()[:, 0:F].rearrange("p (b k) -> p b k", k=16),
                xsq.ap()[:, 0:F].rearrange("p (b k) -> p b k", k=16),
                rbb, Alu.mult)
        v2 = sch.add("vector", c2, waits=[(sDVE, v1b), (sACT, aq)],
                     inc=sDVE)

        def c3(v):
            u = qu.ap()[:, 0:F]
            v.tensor_scalar(qa[:, 0:F], u, TH[0], None, Alu.is_gt)
            v.scalar_tensor_tensor(qb_[:, 0:F], u, TH[1], qa.ap()[:, 0:F],
                                   Alu.is_gt, Alu.add)
            v.scalar_tensor_tensor(qa[:, 0:F], u, TH[2], qb_.ap()[:, 0:F],
                                   Alu.is_gt, Alu.add)
            v.scalar_tensor_tensor(qb_[:, 0:F], u, TH[3], qa.ap()[:, 0:F],
                                   Alu.is_gt, Alu.add)
            v.tensor_scalar(qc_[:, 0:F], u, TH[4], None, Alu.is_gt)
            v.scalar_tensor_tensor(qc_[:, 0:F], u, TH[5], qc_.ap()[:, 0:F],
                                   Alu.is_gt, Alu.add)
            v.scalar_tensor_tensor(qc_[:, 0:F], u, TH[6], qc_.ap()[:, 0:F],
                                   Alu.is_gt, Alu.add)
            v.scalar_tensor_tensor(qc_[:, 0:F], u, TH[6], qc_.ap()[:, 0:F],
                                   Alu.is_gt, Alu.add)
            v.scalar_tensor_tensor(qu[:, 0:F], qc_.ap()[:, 0:F], 2.0,
                                   qb_.ap()[:, 0:F], Alu.mult, Alu.add)
            shb = sh.ap()[:, 0:nb].unsqueeze(2).broadcast_to([128, nb, 16])
            v.tensor_tensor(
                qt_.ap()[:, 0:F].rearrange("p (b k) -> p b k", k=16),
                qu.ap()[:, 0:F].rearrange("p (b k) -> p b k", k=16),
                shb, Alu.mult)
            v.tensor_scalar(s2[:, 0:F], sg[:, 0:F], 2.0, -1.0,
                            Alu.mult, Alu.add)
            return v.tensor_tensor(dst.ap()[:, doff:doff + F],
                                   qt_.ap()[:, 0:F],
                                   s2.ap()[:, 0:F], Alu.mult)
        return sch.add("vector", c3,
                       waits=[(sDVE, v2)] + list(waits3), inc=sDVE)


    with root, nc.Block() as block:
        stTmp = ExitStack()
        idn_f = mksb(stTmp, "idn_f", [128, 128], F32, side=R)
        cos_f = mksb(stTmp, "cos_f", [128, NT, 32], F32, side=R)
        sin_f = mksb(stTmp, "sin_f", [128, NT, 32], F32, side=R)
        msk_f = mksb(stTmp, "msk_f", [128, 4, 512], F32, side=R)
        bst = mksb(stTmp, "bst", [1, NP * 384], F32, side=R)

        stCst = ExitStack()
        cosb = mksb(stCst, "cosb", [128, NT, 32], BF16)
        sinb = mksb(stCst, "sinb", [128, NT, 32], BF16)
        mskb = mksb(stCst, "mskb", [128, 4, 512], BF16)
        bqkvs = mksb(stCst, "bqkvs", [1, NP * 384], BF16)
        bos = mksb(stCst, "bos", [1, D], BF16)

        dma("sync", sCST, cos_f[:], cos_h.rearrange("(n p) j -> p n j", p=128))
        dma("sync", sCST, sin_f[:], sin_h.rearrange("(n p) j -> p n j", p=128))
        dma("sync", sCST, idn_f[:], idn_h)
        w_pre = dma("sync", sCST, msk_f[:],
                    msk_h.rearrange("p (m f) -> p m f", f=512))

        def cast_consts(v):
            v.memset(epsc[:], 1e-5)
            v.memset(nb12[:], -12.0)
            v.memset(ones1[:], 1.0)
            v.tensor_copy(idn[:], idn_f[:])
            v.tensor_copy(cosb[:], cos_f[:])
            v.tensor_copy(sinb[:], sin_f[:])
            return v.tensor_copy(mskb[:], msk_f[:])
        v_consts = sch.add("vector", cast_consts, waits=[(sCST, w_pre)],
                           inc=sDVE)

        wb1 = dma("sync", sBIA, bst[:], bqkv_h)
        ab1 = sch.add("scalar", (lambda c: c.copy(bqkvs[:], bst[:])),
                      waits=[(sBIA, wb1)], inc=sACT)
        wb2 = dma("sync", sBIA, bst.ap()[:, 0:D], bo_h, waits=[(sACT, ab1)])
        sch.add("scalar", (lambda c: c.copy(bos[:], bst.ap()[:, 0:D])),
                waits=[(sBIA, wb2)], inc=sACT)
        bias_cast_done = sACT.n
        sch.run(block)
        stTmp.close()

        stXlT = ExitStack()
        xlT = mksb(stXlT, "xlT", [128, KC, S], BF16, side=R)
        stLN = ExitStack()
        x_a = [mksb(stLN, f"x_a{i}", [128, D], F32, side=R) for i in range(2)]
        sqt = mksb(stLN, "sqt", [128, D], F32, side=R)
        mv = [mksb(stLN, f"mv{i}", [128, 6], F32, side=R) for i in range(2)]
        rinv1 = [mksb(stLN, f"rinv1_{i}", [128, 1], F32, side=R)
                 for i in range(2)]
        rstd1 = [mksb(stLN, f"rstd1_{i}", [128, 1], F32, side=R)
                 for i in range(2)]
        negm1 = [mksb(stLN, f"negm1_{i}", [128, 1], F32, side=R)
                 for i in range(2)]
        xl1 = [mksb(stLN, f"xl1_{i}", [128, D], BF16, side=R) for i in range(2)]

        evacA = {}
        aapA = {}
        vsumA = {}
        for t in range(NT):
            r = t % 2
            wx = []
            if t >= 2:
                wx += [(sACT, aapA[t - 2]), (sDVE, vsumA[t - 2])]
            wxd = dma("sync", sXA[r], x_a[r][:], x_v[:, t, :], waits=wx)
            vsumA[t] = sch.add("vector", (lambda v, r=r: v.reduce_sum(
                mv[r][:, 0:1], x_a[r][:], AX.X)),
                waits=[(sXA[r], wxd)], inc=sDVE)
            asq = sch.add("scalar", (lambda c, r=r: c.activation(
                sqt[:], x_a[r][:], AFT.Square, accum_out=mv[r][:, 1:2])),
                waits=[(sXA[r], wxd)], inc=sACT)

            def stat2(v, r=r):
                v.tensor_scalar(mv[r][:, 2:3], mv[r][:, 0:1], 1.0 / D, None,
                                Alu.mult)
                return v.tensor_scalar(mv[r][:, 3:4], mv[r][:, 1:2], 1.0 / D,
                                       1e-5, Alu.mult, Alu.add)
            v2a = sch.add("vector", stat2,
                          waits=[(sACT, asq), (sDVE, vsumA[t])], inc=sDVE)
            v2b = sch.add("vector", (lambda v, r=r: v.tensor_tensor(
                mv[r][:, 4:5], mv[r][:, 2:3], mv[r][:, 2:3], Alu.mult)),
                waits=[(sDVE, v2a)], inc=sDVE)
            vst = sch.add("vector", (lambda v, r=r: v.tensor_tensor(
                mv[r][:, 5:6], mv[r][:, 3:4], mv[r][:, 4:5], Alu.subtract)),
                waits=[(sDVE, v2b)], inc=sDVE)
            vrc = sch.add("vector", (lambda v, r=r: v.reciprocal(
                rinv1[r][:], mv[r][:, 5:6])), waits=[(sDVE, vst)], inc=sDVE)
            ast = sch.add("scalar", (lambda c, r=r: c.activation(
                rstd1[r][:], rinv1[r][:], AFT.Sqrt)),
                waits=[(sDVE, vrc)], inc=sACT)
            vng = sch.add("vector", (lambda v, r=r: v.tensor_scalar(
                negm1[r][:], mv[r][:, 2:3], rstd1[r].ap(), -1.0,
                Alu.mult, Alu.mult)),
                waits=[(sACT, ast)], inc=sDVE)
            aapA[t] = sch.add("scalar", (lambda c, r=r: c.activation(
                xl1[r][:], x_a[r][:], AFT.Identity,
                bias=negm1[r].ap(), scale=rstd1[r].ap())),
                waits=[(sDVE, vng)], inc=sACT)

            def trsA(te, r=r):
                ins = None
                for j in range(KC):
                    ins = te.transpose(psTs[r][:, j * 128:(j + 1) * 128],
                                       xl1[r][:, j * 128:(j + 1) * 128],
                                       idn[:])
                return ins
            wtr = [(sACT, aapA[t]), (sDVE, v_consts)]
            if t >= 2:
                wtr.append((sACT, evacA[t - 2]))
            ptr = sch.add("tensor", trsA, waits=wtr, inc=sPE)
            evacA[t] = sch.add("scalar", (lambda c, t=t, r=r: c.copy(
                xlT.ap()[:, :, t * 128:(t + 1) * 128],
                psTs[r].ap().rearrange("p (j f) -> p j f", f=128))),
                waits=[(sPE, ptr)], inc=sACT)
        phaseA_act = sACT.n
        sch.run(block)
        stLN.close()

        stEq = ExitStack()
        FQ = 1024
        wfE = [mksb(stEq, f"wfE{i}", [128, FQ], F32, side=R) for i in range(2)]
        quE = mksb(stEq, "quE", [128, FQ], F32, side=R)
        qtE = mksb(stEq, "qtE", [128, FQ], F32, side=R)
        qcA = mksb(stEq, "qcA", [128, FQ], F32, side=R)
        qcB = mksb(stEq, "qcB", [128, FQ], F32, side=R)
        qcC = mksb(stEq, "qcC", [128, FQ], F32, side=R)
        qsg = mksb(stEq, "qsg", [128, FQ], BF16, side=R)
        qs2 = mksb(stEq, "qs2", [128, FQ], BF16, side=R)
        qqE = [mksb(stEq, f"qqE{i}", [128, FQ], BF16, side=R)
               for i in range(2)]
        qbm = mksb(stEq, "qbm", [128, FQ // 16], F32, side=R)
        qsh = mksb(stEq, "qsh", [128, FQ // 16], F32, side=R)
        qrb = mksb(stEq, "qrb", [128, FQ // 16], F32, side=R)
        qbc = mksb(stEq, "qbc", [128, FQ // 16], F32, side=R)
        qr2 = mksb(stEq, "qr2", [128, FQ // 16], F32, side=R)
        qxs = mksb(stEq, "qxs", [128, FQ], F32, side=R)
        bufsE = (quE, qtE, qcA, qcB, qcC, qsg, qs2, qbm, qsh, qrb, qbc,
                 qr2, qxs)

        wjobs = []
        for rt in range(NJ):
            wjobs.append((wg_h, wgq_raw, rt, 0, 1024))
        for rt in range(NJ):
            wjobs.append((wu_h, wuq_raw, rt, 0, 1024))
        for rt in range(KC):
            for cc, F in ((0, 1024), (1024, 1024), (2048, 768)):
                wjobs.append((wd_h, wdq_raw, rt, cc, F))

        e1 = {"ji": 0, "stage2": None, "prev_chain": 0, "out_dma": {},
              "aq_prev": {}}
        a_done_dve = sDVE.n

        def e1_stage1():
            ji = e1["ji"]
            if ji >= len(wjobs):
                return
            wh, raw, rt, cc, F = wjobs[ji]
            r = ji % 2
            ws = ([(sDVE, e1["prev_chain"])] if ji >= 1
                  else [(sACT, phaseA_act), (sDVE, a_done_dve)])
            wdma = dma("sync", sDMA, wfE[r][:, 0:F],
                       wh[rt * 128:(rt + 1) * 128, cc:cc + F], waits=ws)
            e1["stage2"] = (ji, wh, raw, rt, cc, F, r, wdma)
            e1["ji"] = ji + 1

        def e1_do_stage2():
            if e1["stage2"] is None:
                return
            ji, wh, raw, rt, cc, F, r, wdma = e1["stage2"]
            e1["stage2"] = None
            qr = ji % 2
            w3 = []
            if ji >= 2 and (ji - 2) in e1["out_dma"]:
                w3.append((sSCR, e1["out_dma"][ji - 2]))
            qv = quant_chain(bufsE, F, wfE[r], 0, qqE[qr], 0,
                             [(sDMA, wdma)], w3)
            e1["prev_chain"] = qv
            e1["out_dma"][ji] = dma(
                "gpsimd", sSCR, raw[rt * 128:(rt + 1) * 128, cc:cc + F],
                qqE[qr].ap()[:, 0:F], waits=[(sDVE, qv)])

        def e1_pump():
            e1_do_stage2()
            e1_stage1()

        stAttnT = ExitStack()
        attnT = mksb(stAttnT, "attnT", [128, NP, S], BF16)
        stB = ExitStack()
        wstg = mksb(stB, "wstg", [128, KC, 384], F32, side=R)
        wqkvp = mksb(stB, "wqkvp", [128, KC, 384], BF16, side=R)
        q_tok = mksb(stB, "q_tok", [128, NT, 128], BF16, side=R)
        k_tok = mksb(stB, "k_tok", [128, NT, 128], BF16, side=R)
        vaug = mksb(stB, "vaug", [128, NT, 2, 65], BF16, side=R)
        tAb = mksb(stB, "tAb", [128, NT, 2, 2, 32], BF16, side=R)
        tBb = mksb(stB, "tBb", [128, NT, 2, 2, 32], BF16, side=R)
        QTr = mksb(stB, "QTr", [128, S], BF16, side=R)
        KTr = mksb(stB, "KTr", [128, S], BF16, side=R)
        PT = mksb(stB, "PT", [128, 4, 512], BF16, side=R)
        att65 = [mksb(stB, f"att65_{i}", [65, 512], BF16, side=R)
                 for i in range(2)]
        attok = mksb(stB, "attok", [128, NT, 65], BF16, side=R)
        attokn = mksb(stB, "attokn", [128, NT, 64], BF16, side=R)
        rd16 = mksb(stB, "rd16", [128, NT], F32, side=R)

        v_ones = sch.add("vector", (lambda v: v.memset(
            vaug.ap()[:, :, :, 64:65], 1.0)),
            waits=[(sDVE, v_consts)], inc=sDVE)

        sc_banks = [psC, psD]
        av_banks = [psO.ap()[0:65, 0:512], psO.ap()[0:65, 512:1024]]
        sc_last_exp = [0, 0]
        av_last_evac = [0, 0]
        sc_ctr = 0
        wcast_done = 0
        attok_prev = 0
        t2_prev_pe = 0
        for pr in range(NP):
            ws = ([(sDVE, wcast_done)] if wcast_done
                  else [(sACT, phaseA_act)])
            wdma = dma("sync", sWB, wstg[:],
                       wqkv_h[:, pr * 384:(pr + 1) * 384]
                       .rearrange("(c p) n -> p c n", p=128), waits=ws)
            wcast_done = sch.add("vector", (lambda v: v.tensor_copy(
                wqkvp.ap().rearrange("p c n -> p (c n)"),
                wstg.ap().rearrange("p c n -> p (c n)"))),
                waits=[(sWB, wdma)], inc=sDVE)

            evacB = {}
            for t in range(NT):
                ps = [psA, psB][t % 2]

                def qkv(te, t=t, ps=ps, pr=pr):
                    for kc in range(KC):
                        te.matmul(ps[:, 0:384],
                                  xlT.ap()[:, kc, t * 128:(t + 1) * 128],
                                  wqkvp.ap()[:, kc, :],
                                  start=(kc == 0), stop=False)
                    return te.matmul(ps[:, 0:384], ones1.ap(),
                                     bqkvs.ap()[:, pr * 384:(pr + 1) * 384],
                                     start=False, stop=True)
                waits = [(sDVE, wcast_done), (sACT, bias_cast_done),
                         (sACT, phaseA_act)]
                if t >= 2:
                    waits.append((sACT, evacB[t - 2]))
                elif pr > 0:
                    waits.append((sACT, sACT.n))
                pm = sch.add("tensor", qkv, waits=waits, inc=sPE)

                def ev(c, t=t, ps=ps):
                    c.copy(q_tok[:, t, :], ps[:, 0:128])
                    c.copy(k_tok[:, t, :], ps[:, 128:256])
                    return c.copy(
                        vaug.ap()[:, t, :, 0:64],
                        ps.ap()[:, 256:384]
                        .rearrange("p (a l) -> p a l", l=64))
                evacB[t] = sch.add("scalar", ev, waits=[(sPE, pm)], inc=sACT)
            qkv_evac = sACT.n
            e1_pump()

            def rope(v, src):
                sv = src.ap().rearrange("p n (h x j) -> p n h x j",
                                        h=2, x=2, j=32)
                cb = cosb.ap().unsqueeze(2).unsqueeze(3).broadcast_to(
                    [128, NT, 2, 1, 32])
                sbb = sinb.ap().unsqueeze(2).unsqueeze(3).broadcast_to(
                    [128, NT, 2, 1, 32])
                x0 = sv[:, :, :, 0:1, :]
                x1 = sv[:, :, :, 1:2, :]
                v.tensor_tensor(tAb.ap()[:, :, :, 0:1, :], x0, cb, Alu.mult)
                v.tensor_tensor(tBb.ap()[:, :, :, 0:1, :], x1, sbb, Alu.mult)
                v.tensor_tensor(tAb.ap()[:, :, :, 1:2, :], x0, sbb, Alu.mult)
                v.tensor_tensor(tBb.ap()[:, :, :, 1:2, :], x1, cb, Alu.mult)
                v.tensor_tensor(sv[:, :, :, 0:1, :], tAb.ap()[:, :, :, 0:1, :],
                                tBb.ap()[:, :, :, 0:1, :], Alu.subtract)
                return v.tensor_tensor(sv[:, :, :, 1:2, :],
                                       tAb.ap()[:, :, :, 1:2, :],
                                       tBb.ap()[:, :, :, 1:2, :], Alu.add)
            vr_q = sch.add("vector", (lambda v: rope(v, q_tok)),
                           waits=[(sACT, qkv_evac)], inc=sDVE)
            vr_k = sch.add("vector", (lambda v: rope(v, k_tok)),
                           waits=[(sACT, qkv_evac)], inc=sDVE)

            tr_evac = {}
            for which, src, dstT, vw in (("q", q_tok, QTr, vr_q),
                                         ("k", k_tok, KTr, vr_k)):
                for g in range(2):
                    ps = psTs[g % 2]

                    def trs(te, src=src, g=g, ps=ps):
                        ins = None
                        for jj in range(8):
                            t = g * 8 + jj
                            ins = te.transpose(ps[:, jj * 128:(jj + 1) * 128],
                                               src[:, t, :], idn[:])
                        return ins
                    waits = [(sDVE, vw)]
                    if (which, g) == ("q", 0):
                        waits.append((sACT, qkv_evac))
                        if pr == 0:
                            waits.append((sACT, evacA[NT - 1]))
                        else:
                            waits.append((sACT, attok_prev))
                    if tr_evac:
                        waits.append((sACT, max(tr_evac.values())))
                    ptr = sch.add("tensor", trs, waits=waits, inc=sPE)
                    tr_evac[(which, g)] = sch.add("scalar", (
                        lambda c, dstT=dstT, g=g, ps=ps: c.copy(
                            dstT.ap()[:, g * 1024:(g + 1) * 1024], ps[:])),
                        waits=[(sPE, ptr)], inc=sACT)
            trq_done = sACT.n

            for hh in range(2):
                po = 64 * hh
                for qb in range(4):
                    nkc = 4 * qb + 4
                    abk = (pr * 8 + hh * 4 + qb) % 2
                    exp_of = {}
                    av_of = {}
                    pend = []
                    for kc in range(nkc):
                        bk = sc_ctr % 2
                        sc_ctr += 1

                        def smm(te, kc=kc, qb=qb, po=po, ps=sc_banks[bk]):
                            return te.matmul(
                                ps[:],
                                KTr.ap()[po:po + 64, kc * 128:(kc + 1) * 128],
                                QTr.ap()[po:po + 64, qb * 512:(qb + 1) * 512],
                                start=True, stop=True)
                        waits = [(sACT, trq_done)]
                        if sc_last_exp[bk]:
                            waits.append((sACT, sc_last_exp[bk]))
                        pm = sch.add("tensor", smm, waits=waits, inc=sPE)
                        pend.append((kc, bk, pm))
                        if len(pend) >= 2:
                            kcp, bkp, pmp = pend.pop(0)
                            ew = [(sPE, pmp)]
                            if kcp >= 4:
                                ew.append((sPE, av_of[kcp - 4]))
                            exp_of[kcp] = sch.add("scalar", (
                                lambda c, kcp=kcp, ps=sc_banks[bkp]:
                                c.activation(
                                    PT[:, kcp % 4, :], ps[:], AFT.Exp,
                                    bias=nb12.ap(), scale=0.125)),
                                waits=ew, inc=sACT)
                            sc_last_exp[bkp] = exp_of[kcp]
                            mneed = kcp >= 4 * qb
                            vmk = None
                            if mneed:
                                vmk = sch.add("vector", (
                                    lambda v, kcp=kcp, qb=qb: v.tensor_tensor(
                                        PT[:, kcp % 4, :], PT[:, kcp % 4, :],
                                        mskb.ap()[:, kcp - 4 * qb, :],
                                        Alu.mult)),
                                    waits=[(sACT, exp_of[kcp])], inc=sDVE)

                            def avm(te, kcp=kcp, hh=hh, nkc=nkc, abk=abk):
                                return te.matmul(
                                    av_banks[abk], vaug.ap()[:, kcp, hh, :],
                                    PT[:, kcp % 4, :],
                                    start=(kcp == 0), stop=(kcp == nkc - 1),
                                    skip_group_check=True)
                            avw = ([(sDVE, vmk)] if mneed
                                   else [(sACT, exp_of[kcp])])
                            if kcp == 0:
                                avw.append((sDVE, v_ones))
                                if av_last_evac[abk]:
                                    avw.append((sACT, av_last_evac[abk]))
                            av_of[kcp] = sch.add("tensor", avm, waits=avw,
                                                 inc=sPE)
                    while pend:
                        kcp, bkp, pmp = pend.pop(0)
                        ew = [(sPE, pmp)]
                        if kcp >= 4:
                            ew.append((sPE, av_of[kcp - 4]))
                        exp_of[kcp] = sch.add("scalar", (
                            lambda c, kcp=kcp, ps=sc_banks[bkp]:
                            c.activation(
                                PT[:, kcp % 4, :], ps[:], AFT.Exp,
                                bias=nb12.ap(), scale=0.125)),
                            waits=ew, inc=sACT)
                        sc_last_exp[bkp] = exp_of[kcp]
                        mneed = kcp >= 4 * qb
                        vmk = None
                        if mneed:
                            vmk = sch.add("vector", (
                                lambda v, kcp=kcp, qb=qb: v.tensor_tensor(
                                    PT[:, kcp % 4, :], PT[:, kcp % 4, :],
                                    mskb.ap()[:, kcp - 4 * qb, :], Alu.mult)),
                                waits=[(sACT, exp_of[kcp])], inc=sDVE)

                        def avm(te, kcp=kcp, hh=hh, nkc=nkc, abk=abk):
                            return te.matmul(
                                av_banks[abk], vaug.ap()[:, kcp, hh, :],
                                PT[:, kcp % 4, :],
                                start=(kcp == 0), stop=(kcp == nkc - 1),
                                skip_group_check=True)
                        avw = ([(sDVE, vmk)] if mneed
                               else [(sACT, exp_of[kcp])])
                        if kcp == 0:
                            avw.append((sDVE, v_ones))
                            if av_last_evac[abk]:
                                avw.append((sACT, av_last_evac[abk]))
                        av_of[kcp] = sch.add("tensor", avm, waits=avw,
                                             inc=sPE)
                    a65b = att65[abk]
                    av_last_evac[abk] = sch.add("scalar", (
                        lambda c, a65b=a65b, abk=abk: c.copy(
                            a65b[:], av_banks[abk])),
                        waits=[(sPE, av_of[nkc - 1])], inc=sACT)

                    def t1s(te, a65b=a65b):
                        ins = None
                        for qc in range(4):
                            ins = te.transpose(
                                psT1[:, qc * 128:qc * 128 + 65],
                                a65b.ap()[:, qc * 128:(qc + 1) * 128],
                                idn.ap()[0:65, 0:65])
                        return ins
                    wt1 = [(sACT, av_last_evac[abk])]
                    if attok_prev:
                        wt1.append((sACT, attok_prev))
                    pt1 = sch.add("tensor", t1s, waits=wt1, inc=sPE)
                    attok_prev = sch.add("scalar", (lambda c, qb=qb: c.copy(
                        attok.ap()[:, qb * 4:(qb + 1) * 4, :],
                        psT1.ap()[:, 0:512]
                        .rearrange("p (qc f) -> p qc f", f=128)
                        [:, :, 0:65])),
                        waits=[(sPE, pt1)], inc=sACT)
                    if qb >= 1:
                        e1_pump()
                all_attok = sACT.n
                vrd = sch.add("vector", (lambda v: v.reciprocal(
                    rd16[:], attok.ap()[:, :, 64])),
                    waits=[(sACT, all_attok), (sPE, t2_prev_pe)], inc=sDVE)
                vnorm = sch.add("vector", (lambda v: v.tensor_tensor(
                    attokn[:], attok.ap()[:, :, 0:64],
                    rd16.ap().unsqueeze(2).broadcast_to([128, NT, 64]),
                    Alu.mult)), waits=[(sDVE, vrd)], inc=sDVE)
                for g in range(2):
                    def t2s(te, g=g):
                        ins = None
                        for jj in range(8):
                            qt = g * 8 + jj
                            ins = te.transpose(
                                psT2[0:64, jj * 128:(jj + 1) * 128],
                                attokn[:, qt, :], idn[:])
                        return ins
                    wt2 = [(sDVE, vnorm), (sACT, sACT.n)]
                    t2_prev_pe = sch.add("tensor", t2s, waits=wt2, inc=sPE)
                    sch.add("scalar", (lambda c, g=g, po=po, pr=pr: c.copy(
                        attnT.ap()[po:po + 64, pr, g * 1024:(g + 1) * 1024]
                        .rearrange("p (jj f) -> p jj f", f=128),
                        psT2.ap()[0:64, :]
                        .rearrange("p (jj f) -> p jj f", f=128))),
                        waits=[(sPE, t2_prev_pe)], inc=sACT)
                e1_pump()
        stB_attn_done = sACT.n
        stB_dve_done = sDVE.n
        dbg("d_attnT", [128, NP, S], BF16, attnT[:], [(sACT, stB_attn_done)])
        sch.run(block)
        stB.close()

        stC = ExitStack()
        wostg = mksb(stC, "wostg", [128, KC, 512], F32, side=R)
        wos = mksb(stC, "wos", [128, KC, D], BF16, side=R)
        x_c = [mksb(stC, f"x_c{i}", [128, D], F32, side=R) for i in range(2)]
        x1b = [mksb(stC, f"x1b{i}", [128, D], F32, side=R) for i in range(2)]
        sqt2 = mksb(stC, "sqt2", [128, D], F32, side=R)
        mv2 = [mksb(stC, f"mv2_{i}", [128, 6], F32, side=R) for i in range(2)]
        rt2 = [mksb(stC, f"rt2_{i}", [128, 2], F32, side=R) for i in range(2)]

        for nh in range(2):
            wdma = dma("sync", sWO, wostg[:],
                       wo_h[:, nh * 512:(nh + 1) * 512]
                       .rearrange("(c p) n -> p c n", p=128),
                       waits=([(sDVE, sDVE.n)] if nh
                              else [(sACT, stB_attn_done),
                                    (sDVE, stB_dve_done)]))
            sch.add("vector", (lambda v, nh=nh: v.tensor_copy(
                wos.ap()[:, :, nh * 512:(nh + 1) * 512], wostg.ap())),
                waits=[(sWO, wdma)], inc=sDVE)
        wo_cast = sDVE.n

        epiC = {}
        dumpC = {}
        aC = {}
        vC = {}
        for t in range(NT):
            r = t % 2
            wx = [(sACT, stB_attn_done)]
            if t >= 2:
                wx += [(sDVE, vC[t - 2]), (sACT, aC[t - 2])]
            wxd = dma("sync", sXC[r], x_c[r][:], x_v[:, t, :], waits=wx)

            def omm(te, t=t):
                ins = None
                for nh in range(2):
                    for pr in range(NP):
                        te.matmul(psO[:, nh * 512:(nh + 1) * 512],
                                  attnT.ap()[:, pr, t * 128:(t + 1) * 128],
                                  wos.ap()[:, pr, nh * 512:(nh + 1) * 512],
                                  start=(pr == 0), stop=False)
                    ins = te.matmul(psO[:, nh * 512:(nh + 1) * 512],
                                    ones1.ap(),
                                    bos.ap()[:, nh * 512:(nh + 1) * 512],
                                    start=False, stop=True)
                return ins
            waits = [(sACT, stB_attn_done), (sDVE, wo_cast)]
            if t >= 1:
                waits.append((sDVE, epiC[t - 1]))
            pm = sch.add("tensor", omm, waits=waits, inc=sPE)
            epw = [(sPE, pm), (sXC[r], wxd)]
            if t >= 2:
                epw.append((sDP[r], dumpC[t - 2]))
            epiC[t] = sch.add("vector", (lambda v, t=t, r=r: v.tensor_tensor(
                x1b[r][:], x_c[r][:], psO[:], Alu.add)), waits=epw, inc=sDVE)
            dumpC[t] = dma("gpsimd", sDP[r], xscr_v[:, t, :], x1b[r][:],
                           waits=[(sDVE, epiC[t])])
            vsum = sch.add("vector", (lambda v, r=r: v.reduce_sum(
                mv2[r][:, 0:1], x1b[r][:], AX.X)),
                waits=[(sDVE, epiC[t])], inc=sDVE)
            asq = sch.add("scalar", (lambda c, r=r: c.activation(
                sqt2[:], x1b[r][:], AFT.Square, accum_out=mv2[r][:, 1:2])),
                waits=[(sDVE, epiC[t])], inc=sACT)

            def stat2d(v, t=t, r=r):
                v.tensor_scalar(mv2[r][:, 2:3], mv2[r][:, 0:1], 1.0 / D, None,
                                Alu.mult)
                v.tensor_scalar(negm_all[:, t:t + 1], mv2[r][:, 0:1],
                                -1.0 / D, None, Alu.mult)
                return v.tensor_scalar(mv2[r][:, 3:4], mv2[r][:, 1:2],
                                       1.0 / D, 1e-5, Alu.mult, Alu.add)
            v2a = sch.add("vector", stat2d, waits=[(sACT, asq), (sDVE, vsum)],
                          inc=sDVE)
            v2b = sch.add("vector", (lambda v, r=r: v.tensor_tensor(
                mv2[r][:, 4:5], mv2[r][:, 2:3], mv2[r][:, 2:3], Alu.mult)),
                waits=[(sDVE, v2a)], inc=sDVE)
            vst = sch.add("vector", (lambda v, r=r: v.tensor_tensor(
                rt2[r][:, 0:1], mv2[r][:, 3:4], mv2[r][:, 4:5],
                Alu.subtract)), waits=[(sDVE, v2b)], inc=sDVE)
            vC[t] = sch.add("vector", (lambda v, r=r: v.reciprocal(
                rt2[r][:, 1:2], rt2[r][:, 0:1])), waits=[(sDVE, vst)],
                inc=sDVE)
            aC[t] = sch.add("scalar", (lambda c, t=t, r=r: c.activation(
                rstd_all[:, t:t + 1], rt2[r][:, 1:2], AFT.Sqrt)),
                waits=[(sDVE, vC[t])], inc=sACT)
            e1_pump()
            e1_pump()
        while e1["ji"] < len(wjobs) or e1["stage2"] is not None:
            e1_pump()
        e1_done_eo = (sEO[0].n, sEO[1].n)
        e1_done_dve = sDVE.n
        ln2_done_act = sACT.n
        ln2_done_vec = sDVE.n
        xdump_done = (sDP[0].n, sDP[1].n)
        dbg("d_rstd", [128, NT], F32, rstd_all[:], [(sACT, ln2_done_act)])
        dbg("d_negm", [128, NT], F32, negm_all[:], [(sDVE, ln2_done_vec)])
        sch.run(block)
        stC.close()
        stEq.close()
        stXlT.close()
        stAttnT.close()
        stCst.close()

        stE = ExitStack()
        wgqT = mksb(stE, "wgqT", [128, KC, DFF], BF16)
        wuqT = mksb(stE, "wuqT", [128, KC, DFF], BF16)
        wdqT = mksb(stE, "wdqT", [128, NJ, D], BF16)
        stEc = ExitStack()
        ebuf = [mksb(stEc, f"ebuf{i}", [128, DFF], BF16, side=R)
                for i in range(2)]

        tjobs = []
        for rt in range(NJ):
            tjobs.append((wgq_raw, wgqT, "gu", rt, 1024))
        for rt in range(NJ):
            tjobs.append((wuq_raw, wuqT, "gu", rt, 1024))
        for rt in range(KC):
            tjobs.append((wdq_raw, wdqT, "d", rt, DFF))

        prev_evacE = 0
        prev_ldE = {}
        for ji, (raw, dstT, kind, rt, F) in enumerate(tjobs):
            r = ji % 2
            ws = ([(sDVE, ln2_done_vec), (sACT, ln2_done_act),
                   (sEO[0], e1_done_eo[0]), (sEO[1], e1_done_eo[1])]
                  if ji < 2 else [(sACT, prev_ldE[ji - 2])])
            wdma = dma("sync", sEB[r], ebuf[r][:, 0:F],
                       raw[rt * 128:(rt + 1) * 128, 0:F], waits=ws)
            njj = F // 128
            for half in range((njj + 7) // 8):
                j0 = half * 8
                jn = min(8, njj - j0)
                pst = psTs[half % 2]

                def trs(te, r=r, j0=j0, jn=jn, pst=pst):
                    ins = None
                    for jj in range(jn):
                        ins = te.transpose(
                            pst[:, jj * 128:(jj + 1) * 128],
                            ebuf[r][:, (j0 + jj) * 128:(j0 + jj + 1) * 128],
                            idn[:])
                    return ins
                wtr = [(sEB[r], wdma)]
                if prev_evacE:
                    wtr.append((sACT, prev_evacE))
                ptr = sch.add("tensor", trs, waits=wtr, inc=sPE)
                if kind == "d":
                    def ev(c, rt=rt, j0=j0, jn=jn, pst=pst):
                        return c.copy(
                            wdqT.ap()[:, j0:j0 + jn,
                                      rt * 128:(rt + 1) * 128],
                            pst.ap()[:, 0:jn * 128]
                            .rearrange("p (jj f) -> p jj f", f=128))
                else:
                    def ev(c, dstT=dstT, rt=rt, j0=j0, jn=jn, pst=pst):
                        return c.copy(
                            dstT.ap()[:, j0:j0 + jn,
                                      rt * 128:(rt + 1) * 128],
                            pst.ap()[:, 0:jn * 128]
                            .rearrange("p (jj f) -> p jj f", f=128))
                prev_evacE = sch.add("scalar", ev, waits=[(sPE, ptr)],
                                     inc=sACT)
            prev_ldE[ji] = prev_evacE
        wq_done_act = sACT.n
        wq_done_vec = sDVE.n
        wq_done_pe = sPE.n
        dbg("d_wgqT", [128, KC, DFF], BF16, wgqT[:], [(sACT, wq_done_act)])
        dbg("d_wdqT", [128, NJ, D], BF16, wdqT[:], [(sACT, wq_done_act)])
        sch.run(block)
        stEc.close()

        stF = ExitStack()
        x_t = [mksb(stF, f"x_t{i}", [128, D], F32, side=R) for i in range(2)]
        xl2c = mksb(stF, "xl2c", [128, D], F32, side=R)
        xqb = mksb(stF, "xqb", [128, D], BF16, side=R)
        xqT_t = mksb(stF, "xqT_t", [128, KC, 128], BF16, side=R)
        silu_sb = [mksb(stF, f"silu{i}", [128, 512], F32, side=R)
                   for i in range(2)]
        up_sb = [mksb(stF, f"up{i}", [128, 512], F32, side=R)
                 for i in range(2)]
        hmat = [mksb(stF, f"hmat{i}", [128, 512], F32, side=R)
                for i in range(2)]
        hqb = [mksb(stF, f"hqb{i}", [128, 512], BF16, side=R)
               for i in range(2)]
        hqT_t = mksb(stF, "hqT_t", [128, NJ, 128], BF16, side=R)
        outb = [mksb(stF, f"outb{i}", [128, D], F32, side=R) for i in range(2)]
        FQ2 = 512
        bufsF = []
        for i in range(2):
            bufsF.append((
                mksb(stF, f"quF{i}", [128, FQ2], F32, side=R),
                mksb(stF, f"qtF{i}", [128, FQ2], F32, side=R),
                mksb(stF, f"qcAF{i}", [128, FQ2], F32, side=R),
                mksb(stF, f"qcBF{i}", [128, FQ2], F32, side=R),
                mksb(stF, f"qcCF{i}", [128, FQ2], F32, side=R),
                mksb(stF, f"qsgF{i}", [128, FQ2], BF16, side=R),
                mksb(stF, f"qs2F{i}", [128, FQ2], BF16, side=R),
                mksb(stF, f"qbmF{i}", [128, FQ2 // 16], F32, side=R),
                mksb(stF, f"qshF{i}", [128, FQ2 // 16], F32, side=R),
                mksb(stF, f"qrbF{i}", [128, FQ2 // 16], F32, side=R),
                mksb(stF, f"qbcF{i}", [128, FQ2 // 16], F32, side=R),
                mksb(stF, f"qr2F{i}", [128, FQ2 // 16], F32, side=R),
                mksb(stF, f"qxsF{i}", [128, FQ2], F32, side=R),
            ))

        ncl = [(0, 512), (512, 512), (1024, 512), (1536, 512), (2048, 512),
               (2560, 256)]
        g_banks = [psA, psC]
        u_banks = [psB, psD]
        g_last = [0, 0]
        u_last = [0, 0]
        chain_last_tr = {}
        prev_down = 0
        prev_epi = 0
        out_dma = {}
        gi = 0
        for t in range(NT):
            r = t % 2
            wx = [(sDP[0], xdump_done[0]), (sDP[1], xdump_done[1])]
            if t < 2:
                wx += [(sDVE, wq_done_vec), (sACT, wq_done_act),
                       (sPE, wq_done_pe)]
            if t >= 2:
                wx.append((sDVE, prev_epi))
            wxd = dma("sync", sXF[r], x_t[r][:], xscr_v[:, t, :], waits=wx)
            ac = sch.add("scalar", (lambda c, t=t, r=r: c.activation(
                xl2c[:], x_t[r][:], AFT.Identity,
                bias=negm_all.ap()[:, t:t + 1], scale=1.0)),
                waits=[(sXF[r], wxd), (sACT, wq_done_act),
                       (sACT, ln2_done_act)],
                inc=sACT)
            w30 = [(sPE, chain_last_tr[0])] if 0 in chain_last_tr else []
            q1 = quant_chain(bufsF[0], 512, xl2c, 0, xqb, 0,
                             [(sACT, ac)], w30)
            w31 = [(sPE, chain_last_tr[1])] if 1 in chain_last_tr else []
            q2 = quant_chain(bufsF[1], 512, xl2c, 512, xqb, 512, [], w31)

            def trx(te):
                ins = None
                for jj in range(KC):
                    ins = te.transpose(psT1[:, jj * 128:(jj + 1) * 128],
                                       xqb[:, jj * 128:(jj + 1) * 128], idn[:])
                return ins
            wtx = [(sDVE, q2), (sACT, sACT.n)]
            ptx = sch.add("tensor", trx, waits=wtx, inc=sPE)
            chain_last_tr[0] = ptx
            chain_last_tr[1] = ptx
            exq = sch.add("scalar", (lambda c: c.copy(
                xqT_t.ap().rearrange("p c f -> p (c f)"), psT1[:])),
                waits=[(sPE, ptx)], inc=sACT)

            pendF = []

            def plan_tail(ncc, nsz, cset):
                njj = nsz // 128
                pst = psTs[cset]
                qh = pend_qh[(ncc, cset)]

                def trh(te, njj=njj, cset=cset, pst=pst):
                    ins = None
                    for jj in range(njj):
                        ins = te.transpose(
                            pst[:, jj * 128:(jj + 1) * 128],
                            hqb[cset][:, jj * 128:(jj + 1) * 128], idn[:])
                    return ins
                wth = [(sDVE, qh), (sACT, sACT.n)]
                pth = sch.add("tensor", trh, waits=wth, inc=sPE)
                chain_last_tr[cset] = pth
                ehq = sch.add("scalar", (lambda c, ncc=ncc, njj=njj, pst=pst:
                              c.copy(
                    hqT_t.ap()[:, ncc // 128:ncc // 128 + njj, :],
                    pst.ap()[:, 0:njj * 128]
                    .rearrange("p (jj f) -> p jj f", f=128))),
                    waits=[(sPE, pth)], inc=sACT)

                def dmm(te, ncc=ncc, njj=njj):
                    ins = None
                    for jj in range(njj):
                        j = ncc // 128 + jj
                        for nh in range(2):
                            ins = te.matmul(
                                psO[:, nh * 512:(nh + 1) * 512],
                                hqT_t.ap()[:, j, :],
                                wdqT.ap()[:, j, nh * 512:(nh + 1) * 512],
                                start=(j == 0), stop=(j == NJ - 1),
                                skip_group_check=True)
                    return ins
                wd_ = [(sACT, ehq)]
                if ncc == 0 and t >= 1:
                    wd_.append((sDVE, prev_epi))
                return sch.add("tensor", dmm, waits=wd_, inc=sPE)

            pend_qh = {}
            for nci, (ncc, nsz) in enumerate(ncl):
                cset = gi % 2
                gb = g_banks[cset]
                ub = u_banks[cset]
                gi += 1

                def gmm(te, ncc=ncc, nsz=nsz, gb=gb):
                    ins = None
                    for kc in range(KC):
                        ins = te.matmul(gb[:, 0:nsz], xqT_t.ap()[:, kc, :],
                                        wgqT.ap()[:, kc, ncc:ncc + nsz],
                                        start=(kc == 0), stop=(kc == KC - 1))
                    return ins

                def umm(te, ncc=ncc, nsz=nsz, ub=ub):
                    ins = None
                    for kc in range(KC):
                        ins = te.matmul(ub[:, 0:nsz], xqT_t.ap()[:, kc, :],
                                        wuqT.ap()[:, kc, ncc:ncc + nsz],
                                        start=(kc == 0), stop=(kc == KC - 1))
                    return ins
                wg_ = [(sACT, exq)]
                if g_last[cset]:
                    wg_.append((sACT, g_last[cset]))
                pg = sch.add("tensor", gmm, waits=wg_, inc=sPE)
                wu_ = [(sACT, u_last[cset])] if u_last[cset] else []
                pu = sch.add("tensor", umm, waits=wu_, inc=sPE)
                asl = sch.add("scalar", (lambda c, t=t, nsz=nsz, gb=gb,
                                         cset=cset: c.activation(
                    silu_sb[cset][:, 0:nsz], gb[:, 0:nsz], AFT.Silu,
                    scale=rstd_all.ap()[:, t:t + 1])),
                    waits=[(sPE, pg)], inc=sACT)
                g_last[cset] = asl
                aup = sch.add("scalar", (lambda c, t=t, nsz=nsz, ub=ub,
                                         cset=cset: c.activation(
                    up_sb[cset][:, 0:nsz], ub[:, 0:nsz], AFT.Identity,
                    scale=rstd_all.ap()[:, t:t + 1])),
                    waits=[(sPE, pu)], inc=sACT)
                u_last[cset] = aup
                vh = sch.add("vector", (lambda v, nsz=nsz, cset=cset:
                             v.tensor_tensor(
                    hmat[cset][:, 0:nsz], silu_sb[cset][:, 0:nsz],
                    up_sb[cset][:, 0:nsz], Alu.mult)),
                    waits=[(sACT, asl), (sACT, aup)], inc=sDVE)
                w3h = ([(sPE, chain_last_tr[cset])]
                       if cset in chain_last_tr else [])
                qh = quant_chain(bufsF[cset], nsz, hmat[cset], 0, hqb[cset],
                                 0, [(sDVE, vh)], w3h)
                pend_qh[(ncc, cset)] = qh
                if pendF:
                    prev_down = plan_tail(*pendF.pop(0))
                pendF.append((ncc, nsz, cset))
            while pendF:
                prev_down = plan_tail(*pendF.pop(0))
            epw = [(sPE, prev_down)]
            if t >= 2:
                epw.append((sFO[r], out_dma[t - 2]))
            prev_epi = sch.add("vector", (lambda v, r=r: v.tensor_tensor(
                outb[r][:], x_t[r][:], psO[:], Alu.add)), waits=epw, inc=sDVE)
            out_dma[t] = dma("gpsimd", sFO[r], out_v[:, t, :], outb[r][:],
                             waits=[(sDVE, prev_epi)])
        final_out = (sFO[0].n, sFO[1].n)

        def final_wait(c):
            c.wait_ge(sFO[0].h, final_out[0])
            return c.wait_ge(sFO[1].h, final_out[1])
        sch.add("sync", final_wait, waits=[])
        sch.run(block)
        stF.close()
        stE.close()
        nc._dbg_outs = dbg_outs
    return nc


_NC_CACHE = {}


def _get_nc():
    if "nc" not in _NC_CACHE:
        _NC_CACHE["nc"] = build_nc()
    return _NC_CACHE["nc"]


def _rope_tables():
    inv = 1.0 / 10000.0 ** (2.0 * np.arange(32, dtype=np.float64) / 64)
    pos = np.arange(S, dtype=np.float64)
    ang = pos[:, None] * inv[None, :]
    return (np.cos(ang).astype(np.float32), np.sin(ang).astype(np.float32))


def _perm():
    p = np.zeros(D, dtype=np.int64)
    for h in range(H):
        for j in range(32):
            p[h * 64 + j] = h * 64 + 2 * j
            p[h * 64 + 32 + j] = h * 64 + 2 * j + 1
    return p


def _masks():
    out = np.zeros((128, 4, 512), dtype=np.float32)
    kt = np.arange(128)[:, None]
    qt = np.arange(512)[None, :]
    for m in range(4):
        out[:, m, :] = (qt >= m * 128 + kt).astype(np.float32)
    return out.reshape(128, 4 * 512)


def kernel(L0_attn_x, L0_attn_Wq, L0_attn_bq, L0_attn_Wk, L0_attn_bk,
           L0_attn_Wv, L0_attn_bv, L0_attn_Wo, L0_attn_bo,
           L0_ffn_w_gate, L0_ffn_w_up, L0_ffn_w_down):
    x = np.ascontiguousarray(np.asarray(L0_attn_x, dtype=np.float32))
    perm = _perm()
    wq = np.asarray(L0_attn_Wq, dtype=np.float32)[:, perm]
    wk = np.asarray(L0_attn_Wk, dtype=np.float32)[:, perm]
    bq = np.asarray(L0_attn_bq, dtype=np.float32)[perm]
    bk = np.asarray(L0_attn_bk, dtype=np.float32)[perm]
    wv = np.asarray(L0_attn_Wv, dtype=np.float32)
    bv = np.asarray(L0_attn_bv, dtype=np.float32)
    wqkv = np.empty((D, NP * 384), dtype=np.float32)
    bqkv = np.empty((1, NP * 384), dtype=np.float32)
    for pr in range(NP):
        cs = slice(pr * 128, (pr + 1) * 128)
        wqkv[:, pr * 384 + 0:pr * 384 + 128] = wq[:, cs]
        wqkv[:, pr * 384 + 128:pr * 384 + 256] = wk[:, cs]
        wqkv[:, pr * 384 + 256:pr * 384 + 384] = wv[:, cs]
        bqkv[0, pr * 384 + 0:pr * 384 + 128] = bq[cs]
        bqkv[0, pr * 384 + 128:pr * 384 + 256] = bk[cs]
        bqkv[0, pr * 384 + 256:pr * 384 + 384] = bv[cs]
    wo = np.ascontiguousarray(np.asarray(L0_attn_Wo, dtype=np.float32))
    bo = np.asarray(L0_attn_bo, dtype=np.float32).reshape(1, D).copy()
    wg = np.ascontiguousarray(np.asarray(L0_ffn_w_gate, dtype=np.float32))
    wu = np.ascontiguousarray(np.asarray(L0_ffn_w_up, dtype=np.float32))
    wd = np.ascontiguousarray(np.asarray(L0_ffn_w_down, dtype=np.float32))
    cosv, sinv = _rope_tables()
    idn = np.eye(128, dtype=np.float32)
    msk = _masks()

    nc = _get_nc()
    in_maps = []
    for b in range(8):
        in_maps.append({
            "x": np.ascontiguousarray(x[b]),
            "wqkv": wqkv, "wo": wo, "bqkv": bqkv, "bo": bo,
            "wg": wg, "wu": wu, "wd": wd,
            "cosv": cosv, "sinv": sinv, "idn": idn, "msk": msk,
        })
    res = run_bass_kernel_spmd(nc, in_maps, core_ids=list(range(8)))
    return np.stack([np.asarray(r["out"], dtype=np.float32)
                     for r in res.results], axis=0)
